# version 10
# baseline (speedup 1.0000x reference)
import numpy as np
import ml_dtypes
import concourse.bass as bass
import concourse.mybir as mybir
from concourse.bass_utils import run_bass_kernel_spmd

F32 = mybir.dt.float32
BF16 = mybir.dt.bfloat16
I8 = mybir.dt.int8
AF = mybir.ActivationFunctionType
ALU = mybir.AluOpType
AX = mybir.AxisListType.X

D = 2048
T = 2048
INTER = 8192
ND = 16
NE = 16
NT = 16
NT4 = 4
NM = 64
NFG = 4
NOB = 16
DEN = float(np.float32(np.sqrt(np.float32(2048.0))) / np.float32(4096.0))
EPS = 1e-5


def _wge(eng, sem, v):
    if v > 0:
        eng.wait_ge(sem, v)


def build_nc(ph_max=9, ph0_sub=3, ph1_sub=3, debug=False):
    nc = bass.Bass("TRN2", target_bir_lowering=False, detect_race_conditions=False)

    x = nc.dram_tensor("x", [T, D], F32, kind="ExternalInput").ap()
    wq_p = nc.dram_tensor("wq_p", [NE, 128, ND, 128], F32, kind="ExternalInput").ap()
    wk_p = nc.dram_tensor("wk_p", [NE, 128, ND, 128], F32, kind="ExternalInput").ap()
    wv_p = nc.dram_tensor("wv_p", [NE, 128, ND, 128], F32, kind="ExternalInput").ap()
    wo_p = nc.dram_tensor("wo_p", [NFG, 128, NE, 512], F32, kind="ExternalInput").ap()
    wg_p = nc.dram_tensor("wg_p", [NM, 128, ND, 128], I8, kind="ExternalInput").ap()
    wu_p = nc.dram_tensor("wu_p", [NM, 128, ND, 128], I8, kind="ExternalInput").ap()
    wd_p = nc.dram_tensor("wd_p", [NFG, NOB, 128, 4, 512], I8, kind="ExternalInput").ap()
    sgt = nc.dram_tensor("sgt", [128, NM], F32, kind="ExternalInput").ap()
    sut = nc.dram_tensor("sut", [128, NM], F32, kind="ExternalInput").ap()
    sdr = nc.dram_tensor("sdr", [128, D], F32, kind="ExternalInput").ap()
    sel = nc.dram_tensor("sel", [8, 1], F32, kind="ExternalInput").ap()
    ident = nc.dram_tensor("ident", [128, 128], BF16, kind="ExternalInput").ap()
    out = nc.dram_tensor("out", [T, D], F32, kind="ExternalOutput").ap()

    _skind = "ExternalOutput" if debug else "Internal"
    fuse_d = nc.dram_tensor("fuse_d", [NT4, 128, NM, 512], BF16, kind=_skind).ap()
    s_d = nc.dram_tensor("s_d", [T, D], F32, kind=_skind).ap()
    if debug:
        dbg_xn = nc.dram_tensor("dbg_xn", [128, ND, T], BF16, kind="ExternalOutput").ap()
        dbg_mix = nc.dram_tensor("dbg_mix", [128, NE, T], BF16, kind="ExternalOutput").ap()
        dbg_vhat = nc.dram_tensor("dbg_vhat", [128, NE, T], BF16, kind="ExternalOutput").ap()
        dbg_car = nc.dram_tensor("dbg_car", [128, NE], F32, kind="ExternalOutput").ap()
        dbg_sem = S_dbg = None
    cin = nc.dram_tensor("cin", [128, NE], F32)
    cout = nc.dram_tensor("cout", [8 * 128, NE], F32)

    from contextlib import ExitStack
    ctx = ExitStack()
    S = lambda n: ctx.enter_context(nc.semaphore(n))

    s_small = S("s_small")
    ld_x = (S("ld_x0"), S("ld_x1"))
    ph0_stat = S("ph0_stat"); ph0_sqrt = S("ph0_sqrt"); ph0_norm = S("ph0_norm")
    ph0_sq2 = S("ph0_sq2")
    ph0_tr = S("ph0_tr"); ph0_evac = S("ph0_evac")
    ld_wkv = (S("ld_wkv0"), S("ld_wkv1"))
    pe_ph1 = S("pe_ph1"); act_ph1 = S("act_ph1")
    dve_ph1 = S("dve_ph1"); dve_scan = S("dve_scan"); d_cin = S("d_cin"); cc = S("cc")
    ld_wgu = (S("ld_wgu0"), S("ld_wgu1"), S("ld_wgu2"), S("ld_wgu3"))
    pe_ph4 = S("pe_ph4"); act_ph4 = S("act_ph4")
    dve_ph4u = S("dve_ph4u"); dve_ph4f = S("dve_ph4f")
    st_fuse = (S("st_fuse0"), S("st_fuse1"))
    d_gath = S("d_gath"); pe_car = S("pe_car"); act_car = S("act_car")
    ld_wq = (S("ld_wq0"), S("ld_wq1"))
    pe_q = S("pe_q"); act_q = S("act_q"); dve_stt = S("dve_stt")
    ld_wo = (S("ld_wo0"), S("ld_wo1"))
    pe_s = S("pe_s"); act_s = S("act_s")
    st_s = (S("st_s0"), S("st_s1"), S("st_s2"), S("st_s3"))
    ld_fuse = (S("ld_fuse0"), S("ld_fuse1"))
    ld_wd = (S("ld_wd0"), S("ld_wd1"))
    pe_dnblk = S("pe_dnblk")
    dve_tmp = S("dve_tmp"); act_g = S("act_g"); dve_out = S("dve_out")
    ld_sx = (S("ld_sx0"), S("ld_sx1"))
    st_out = (S("st_out0"), S("st_out1"))

    sgt_sb = ctx.enter_context(nc.sbuf_tensor("sgt_sb", [128, NM], F32))
    sut_sb = ctx.enter_context(nc.sbuf_tensor("sut_sb", [128, NM], F32))
    sdb = ctx.enter_context(nc.sbuf_tensor("sdb", [128, D], F32))
    ident_sb = ctx.enter_context(nc.sbuf_tensor("ident_sb", [128, 128], BF16))
    sel_sb = ctx.enter_context(nc.sbuf_tensor("sel_sb", [8, 1], F32))
    h_sb = ctx.enter_context(nc.sbuf_tensor("h_sb", [128, NE], F32))
    carry = ctx.enter_context(nc.sbuf_tensor("carry", [128, NE], F32))

    mix_guard = nc.sbuf_tensor("mix", [128, NE, T], BF16)
    mix = mix_guard.__enter__()
    with (
        nc.sbuf_tensor("xn", [128, ND, T], BF16) as xn,
        nc.sbuf_tensor("wgb", [128, 4, ND, 128], BF16) as wgb,
        nc.sbuf_tensor("wub", [128, 4, ND, 128], BF16) as wub,
    ):
        with (
            nc.sbuf_tensor("xt", [128, 2, D], F32) as xt,
            nc.sbuf_tensor("xnt", [128, 2, D], BF16) as xnt,
            nc.sbuf_tensor("st1", [128, 2], F32) as st1,
            nc.sbuf_tensor("stq", [128, 2], F32) as stq,
            nc.sbuf_tensor("stnm", [128, 2], F32) as stnm,
            nc.sbuf_tensor("stv", [128, 2], F32) as stv,
            nc.sbuf_tensor("stsd", [128, 2], F32) as stsd,
            nc.sbuf_tensor("strs", [128, 2], F32) as strs,
            nc.sbuf_tensor("stw", [128, 2], F32) as stw,
            nc.psum_tensor("ptr0", [128, 4, 128], BF16) as ptr0,
            nc.psum_tensor("ptr1", [128, 4, 128], BF16) as ptr1,
            nc.Block() as b0,
        ):
            ptr = (ptr0, ptr1)

            @b0.sync
            def _(sy):
                sy.dma_start(out=sdb[:, :], in_=sdr[:, :]).then_inc(s_small, 16)
                sy.dma_start(out=sgt_sb[:, :], in_=sgt[:, :]).then_inc(s_small, 16)
                sy.dma_start(out=sut_sb[:, :], in_=sut[:, :]).then_inc(s_small, 16)
                sy.dma_start(out=sel_sb[:, :], in_=sel[:, :]).then_inc(s_small, 16)
                sy.dma_start(out=ident_sb[:, :], in_=ident[:, :]).then_inc(s_small, 16)
                for i in range(NT):
                    _wge(sy, ph0_norm, i - 1)
                    _wge(sy, ph0_sq2, 2 * (i - 1))
                    sy.dma_start(out=xt[:, i % 2, :],
                                 in_=x[128 * i:128 * (i + 1), :]).then_inc(ld_x[i % 2], 16)

            @b0.vector
            def _(v):
                for i in range(NT):
                    p = i % 2
                    v.wait_ge(ld_x[p], 16 * (i // 2 + 1))
                    _wge(v, ph0_sqrt, i - 1)
                    v.wait_ge(ph0_sq2, 2 * (i + 1))
                    v.tensor_scalar_mul(stnm[:, p:p + 1], st1[:, p:p + 1], -1.0 / D)
                    v.tensor_mul(stw[:, p:p + 1], stnm[:, p:p + 1], stnm[:, p:p + 1])
                    v.tensor_scalar_mul(stv[:, p:p + 1], stq[:, p:p + 1], 1.0 / D)
                    v.tensor_sub(stv[:, p:p + 1], stv[:, p:p + 1], stw[:, p:p + 1])
                    v.tensor_scalar_add(stv[:, p:p + 1], stv[:, p:p + 1],
                                        EPS).then_inc(ph0_stat, 1)
                    v.wait_ge(ph0_sqrt, i + 1)
                    v.reciprocal(strs[:, p:p + 1], stsd[:, p:p + 1])
                    if ph0_sub >= 2:
                        _wge(v, ph0_tr, 4 * (i - 1))
                    v.tensor_scalar(xnt[:, p, :], xt[:, p, :], stnm[:, p:p + 1],
                                    strs[:, p:p + 1], ALU.add, ALU.mult).then_inc(ph0_norm, 1)
                    if ph0_sub >= 2 and i > 0:
                        for G in range(4):
                            gidx = 4 * (i - 1) + G
                            v.wait_ge(ph0_tr, gidx + 1)
                            ins = None
                            for j in range(4):
                                ins = v.tensor_copy(xn[:, 4 * G + j, 128 * (i - 1):128 * i],
                                                    ptr[gidx % 2][:, j, :])
                            ins.then_inc(ph0_evac, 1)
                if ph0_sub >= 2:
                    for G in range(4):
                        gidx = 4 * 15 + G
                        v.wait_ge(ph0_tr, gidx + 1)
                        ins = None
                        for j in range(4):
                            ins = v.tensor_copy(xn[:, 4 * G + j, 128 * 15:128 * 16],
                                                ptr[gidx % 2][:, j, :])
                        ins.then_inc(ph0_evac, 1)

            @b0.scalar
            def _(sc):
                sc.preload_act_table(AF.Sqrt)

                def sq2_i(i):
                    p = i % 2
                    sc.wait_ge(ld_x[p], 16 * (i // 2 + 1))
                    _wge(sc, ph0_stat, i - 1)
                    _wge(sc, ph0_tr, 4 * (i - 1))
                    sc.activation(out=xnt[:, p, :], in_=xt[:, p, :], func=AF.Square,
                                  accum_out=stq[:, p:p + 1]).then_inc(ph0_sq2, 1)
                    sc.activation(out=xnt[:, p, :], in_=xt[:, p, :], func=AF.Copy,
                                  accum_out=st1[:, p:p + 1]).then_inc(ph0_sq2, 1)

                def sqrt_i(i):
                    p = i % 2
                    sc.wait_ge(ph0_stat, i + 1)
                    _wge(sc, ph0_norm, i - 1)
                    sc.activation(out=stsd[:, p:p + 1], in_=stv[:, p:p + 1],
                                  func=AF.Sqrt).then_inc(ph0_sqrt, 1)

                sq2_i(0)
                sqrt_i(0)
                for i in range(NT):
                    if i + 1 < NT:
                        sq2_i(i + 1)
                        sqrt_i(i + 1)


            @b0.tensor
            def _(t):
                t.wait_ge(s_small, 80)
                if ph0_sub >= 2:
                    for i in range(NT):
                        p = i % 2
                        t.wait_ge(ph0_norm, i + 1)
                        for G in range(4):
                            gidx = 4 * i + G
                            _wge(t, ph0_evac, gidx - 1)
                            ins = None
                            for j in range(4):
                                n = 4 * G + j
                                ins = t.transpose(ptr[gidx % 2][:, j, :],
                                                  xnt[:, p, 128 * n:128 * (n + 1)],
                                                  ident_sb[:, :])
                            ins.then_inc(ph0_tr, 1)

        with (
            nc.sbuf_tensor("kf32", [128, 2, T], BF16) as kf32,
            nc.sbuf_tensor("kvf32", [128, 2, T], BF16) as kvf32,
            nc.sbuf_tensor("wkb", [128, 2, ND, 128], BF16) as wkb,
            nc.sbuf_tensor("wvb", [128, 2, ND, 128], BF16) as wvb,
            nc.psum_tensor("kpA", [128, 2, 512], F32) as kpA,
            nc.psum_tensor("vpA", [128, 2, 512], F32) as vpA,
            nc.psum_tensor("kpB", [128, 2, 512], F32) as kpB,
            nc.psum_tensor("vpB", [128, 2, 512], F32) as vpB,
            nc.Block() as b1,
        ):
            kp = (kpA, kpB); vp = (vpA, vpB)

            @b1.gpsimd
            def _(g):
                if ph_max < 1:
                    return
                for e in range(NE):
                    p = e % 2
                    _wge(g, pe_ph1, 2 * (e - 1))
                    g.dma_start(out=wkb[:, p], in_=wk_p[e]).then_inc(ld_wkv[p], 16)
                    g.dma_start(out=wvb[:, p], in_=wv_p[e]).then_inc(ld_wkv[p], 16)
                if ph_max >= 2:
                    for m in range(4):
                        g.dma_start(out=wgb[:, m], in_=wg_p[m]).then_inc(ld_wgu[m], 16)
                        g.dma_start(out=wub[:, m], in_=wu_p[m]).then_inc(ld_wgu[m], 16)

            @b1.tensor
            def _(t):
                if ph_max < 1:
                    return
                t.wait_ge(ph0_evac, 64)
                for e in range(NE):
                    p = e % 2
                    t.wait_ge(ld_wkv[p], 32 * (e // 2 + 1))
                    for g2 in range(2):
                        if ph1_sub >= 4:
                            _wge(t, act_ph1, 2 * e + g2)
                            _wge(t, dve_ph1, 2 * e + g2)
                        else:
                            _wge(t, act_ph1, 2 * (e - 1) + g2 + 1)
                            _wge(t, dve_ph1, 2 * (e - 1) + g2 + 1)
                        base = 1024 * g2
                        for d in range(ND):
                            for tt in range(2):
                                t.matmul(kp[g2][:, tt, :], wkb[:, p, d, :],
                                         xn[:, d, base + 512 * tt: base + 512 * (tt + 1)],
                                         start=(d == 0), stop=(d == ND - 1))
                        ins = None
                        for d in range(ND):
                            for tt in range(2):
                                ins = t.matmul(vp[g2][:, tt, :], wvb[:, p, d, :],
                                               xn[:, d, base + 512 * tt: base + 512 * (tt + 1)],
                                               start=(d == 0), stop=(d == ND - 1))
                        ins.then_inc(pe_ph1, 1)

            @b1.scalar
            def _(sc):
                if ph_max < 1:
                    return
                for e in range(NE):
                    p = e % 2
                    _wge(sc, dve_ph1, 2 * (e - 2) + 2)
                    for g2 in range(2):
                        sc.wait_ge(pe_ph1, 2 * e + g2 + 1)
                        base = 1024 * g2
                        sc.copy(out=kf32[:, p, base:base + 512], in_=kp[g2][:, 0, :])
                        sc.copy(out=kf32[:, p, base + 512:base + 1024],
                                in_=kp[g2][:, 1, :]).then_inc(act_ph1, 1)
                sc.wait_ge(dve_scan, NE + 1)
                sc.dma_start(out=cin.ap()[:, :], in_=h_sb[:, :]).then_inc(d_cin, 16)

            @b1.vector
            def _(v):
                if ph_max < 1:
                    return
                for e in range(NE):
                    p = e % 2
                    for g2 in range(2):
                        v.wait_ge(pe_ph1, 2 * e + g2 + 1)
                        v.wait_ge(act_ph1, 2 * e + g2 + 1)
                        base = 1024 * g2
                        if ph1_sub >= 2:
                            v.tensor_mul(kvf32[:, p, base:base + 512],
                                         kf32[:, p, base:base + 512], vp[g2][:, 0, :])
                            v.tensor_mul(kvf32[:, p, base + 512:base + 1024],
                                         kf32[:, p, base + 512:base + 1024],
                                         vp[g2][:, 1, :]).then_inc(dve_ph1, 1)
                        else:
                            v.tensor_copy(kvf32[:, p, base:base + 1024],
                                          kf32[:, p, base:base + 1024]).then_inc(dve_ph1, 1)
                    if ph1_sub >= 3:
                        v.tensor_tensor_scan(mix[:, e, :], kvf32[:, p, :], kvf32[:, p, :],
                                             0.0, ALU.add, ALU.bypass)
                    else:
                        v.tensor_copy(mix[:, e, :], kvf32[:, p, :])
                    v.reduce_sum(h_sb[:, e:e + 1], kvf32[:, p, :],
                                 axis=AX).then_inc(dve_scan, 1)
                v.memset(kvf32[:, 0, 0:8], 0.0).then_inc(dve_scan, 1)

        if debug and ph_max >= 1:
            with nc.semaphore("dbg1") as dbg1, nc.Block() as bdbg1:
                @bdbg1.sync
                def _(sy):
                    sy.wait_ge(ph0_evac, 64)
                    sy.wait_ge(dve_scan, NE)
                    sy.dma_start(out=dbg_xn[:, :, :], in_=xn[:, :, :]).then_inc(dbg1, 16)
                    sy.dma_start(out=dbg_mix[:, :, :], in_=mix[:, :, :]).then_inc(dbg1, 16)
                    sy.wait_ge(dbg1, 32)

        ctx_wqb = nc.sbuf_tensor("wqb", [128, 2, ND, 128], BF16)
        wqb = ctx_wqb.__enter__()
        with (
            nc.sbuf_tensor("gf32", [128, 2, 512], F32) as gf32,
            nc.sbuf_tensor("uf32", [128, 2, 512], F32) as uf32,
            nc.sbuf_tensor("fstage", [128, 2, NT4, 512], BF16) as fstage,
            nc.psum_tensor("pg", [128, 4, 512], F32) as pg,
            nc.psum_tensor("pu", [128, 4, 512], F32) as pu,
            nc.Block() as b4,
        ):
            @b4.gpsimd
            def _(g):
                if ph_max < 2:
                    return
                g.wait_ge(d_cin, 16)
                g.collective_compute(
                    "AllGather", ALU.bypass,
                    replica_groups=[list(range(8))],
                    ins=[cin.ap().opt()], outs=[cout.ap().opt()],
                ).then_inc(cc, 1)
                for m in range(4, NM):
                    p = m % 4
                    _wge(g, pe_ph4, 4 * (m - 3))
                    g.dma_start(out=wgb[:, p], in_=wg_p[m]).then_inc(ld_wgu[p], 16)
                    g.dma_start(out=wub[:, p], in_=wu_p[m]).then_inc(ld_wgu[p], 16)
                if ph_max >= 3:
                    for e in range(2):
                        g.dma_start(out=wqb[:, e], in_=wq_p[e]).then_inc(ld_wq[e], 16)

            @b4.tensor
            def _(t):
                if ph_max < 2:
                    return
                for m in range(NM):
                    p = m % 4
                    t.wait_ge(ld_wgu[p], 32 * (m // 4 + 1))
                    for t4 in range(NT4):
                        idx = 4 * m + t4
                        r = idx % 4
                        _wge(t, act_ph4, idx - 3)
                        _wge(t, dve_ph4u, idx - 3)
                        cols = slice(512 * t4, 512 * (t4 + 1))
                        for d in range(ND):
                            t.matmul(pg[:, r, :], wgb[:, p, d, :], xn[:, d, cols],
                                     start=(d == 0), stop=(d == ND - 1))
                        ins = None
                        for d in range(ND):
                            ins = t.matmul(pu[:, r, :], wub[:, p, d, :], xn[:, d, cols],
                                           start=(d == 0), stop=(d == ND - 1))
                        ins.then_inc(pe_ph4, 1)

            @b4.scalar
            def _(sc):
                if ph_max < 2:
                    return
                sc.preload_act_table(AF.Silu)
                for m in range(NM):
                    for t4 in range(NT4):
                        idx = 4 * m + t4
                        r = idx % 4
                        sc.wait_ge(pe_ph4, idx + 1)
                        _wge(sc, dve_ph4f, idx - 1)
                        sc.activation(out=gf32[:, t4 % 2, :], in_=pg[:, r, :],
                                      func=AF.Silu,
                                      scale=sgt_sb[:, m:m + 1]).then_inc(act_ph4, 1)

            @b4.vector
            def _(v):
                if ph_max < 2:
                    return
                for m in range(NM):
                    p = m % 2
                    for t4 in range(NT4):
                        idx = 4 * m + t4
                        r = idx % 4
                        v.wait_ge(pe_ph4, idx + 1)
                        v.tensor_scalar_mul(uf32[:, t4 % 2, :], pu[:, r, :],
                                            sut_sb[:, m:m + 1]).then_inc(dve_ph4u, 1)
                        v.wait_ge(act_ph4, idx + 1)
                        if t4 == 0:
                            _wge(v, st_fuse[p], 64 * (m // 2))
                        v.tensor_mul(fstage[:, p, t4, :], gf32[:, t4 % 2, :],
                                     uf32[:, t4 % 2, :]).then_inc(dve_ph4f, 1)

            @b4.sync
            def _(sy):
                if ph_max < 2:
                    return
                for m in range(NM):
                    p = m % 2
                    for t4 in range(NT4):
                        idx = 4 * m + t4
                        sy.wait_ge(dve_ph4f, idx + 1)
                        sy.dma_start(out=fuse_d[t4][:, m, :],
                                     in_=fstage[:, p, t4, :]).then_inc(st_fuse[p], 16)

        with (
            nc.sbuf_tensor("qsb", [128, 2, T], BF16) as qsb,
            nc.sbuf_tensor("gath", [8, 128, NE], F32) as gath,
            nc.psum_tensor("pcar", [128, NE], F32) as pcar,
            nc.psum_tensor("pq", [128, 4, 512], F32) as pq,
            nc.Block() as b3,
        ):
            @b3.gpsimd
            def _(g):
                if ph_max < 3:
                    return
                for e in range(2, NE):
                    p = e % 2
                    _wge(g, pe_q, 4 * (e - 1))
                    g.dma_start(out=wqb[:, p], in_=wq_p[e]).then_inc(ld_wq[p], 16)


            @b3.scalar
            def _(sc):
                if ph_max < 3:
                    return
                sc.wait_ge(cc, 1)
                sc.dma_start(out=gath[:, :, :],
                             in_=cout.ap().rearrange("(r p) c -> r p c", p=128)
                             ).then_inc(d_gath, 16)
                sc.wait_ge(pe_car, 1)
                sc.copy(out=carry[:, :], in_=pcar[:, :]).then_inc(act_car, 1)
                for e in range(NE):
                    p = e % 2
                    _wge(sc, dve_stt, e - 1)
                    for t4 in range(NT4):
                        idx = 4 * e + t4
                        sc.wait_ge(pe_q, idx + 1)
                        sc.activation(out=qsb[:, p, 512 * t4:512 * (t4 + 1)],
                                      in_=pq[:, t4, :], func=AF.Copy,
                                      scale=DEN).then_inc(act_q, 1)

            @b3.tensor
            def _(t):
                if ph_max < 3:
                    return
                t.wait_ge(d_gath, 16)
                t.wait_ge(s_small, 80)
                ins = None
                for c in range(NE):
                    ins = t.matmul(pcar[:, c:c + 1], gath[:, :, c], sel_sb[:, :],
                                   start=True, stop=True)
                ins.then_inc(pe_car, 1)
                for e in range(NE):
                    p = e % 2
                    t.wait_ge(ld_wq[p], 16 * (e // 2 + 1))
                    for t4 in range(NT4):
                        idx = 4 * e + t4
                        _wge(t, act_q, idx - 3)
                        cols = slice(512 * t4, 512 * (t4 + 1))
                        ins = None
                        for d in range(ND):
                            ins = t.matmul(pq[:, t4, :], wqb[:, p, d, :], xn[:, d, cols],
                                           start=(d == 0), stop=(d == ND - 1))
                        ins.then_inc(pe_q, 1)

            @b3.vector
            def _(v):
                if ph_max < 3:
                    return
                v.wait_ge(act_car, 1)
                for e in range(NE):
                    p = e % 2
                    v.wait_ge(act_q, 4 * e + 4)
                    v.scalar_tensor_tensor(mix[:, e, :], mix[:, e, :], carry[:, e:e + 1],
                                           qsb[:, p, :], ALU.add,
                                           ALU.mult).then_inc(dve_stt, 1)

        ctx_wqb.__exit__(None, None, None)

    if debug and ph_max >= 3:
        with nc.semaphore("dbg2") as dbg2, nc.Block() as bdbg2:
            @bdbg2.sync
            def _(sy):
                sy.wait_ge(dve_stt, NE)
                sy.wait_ge(act_car, 1)
                sy.dma_start(out=dbg_vhat[:, :, :], in_=mix[:, :, :]).then_inc(dbg2, 16)
                sy.dma_start(out=dbg_car[:, :], in_=carry[:, :]).then_inc(dbg2, 16)
                sy.wait_ge(dbg2, 32)

    fuse0 = ctx.enter_context(nc.sbuf_tensor("fuse0", [128, NM, 512], BF16, side="right"))
    wdb = ctx.enter_context(nc.sbuf_tensor("wdb", [128, 2, 4, 512], BF16, side="right"))
    if True:
        with (
            nc.sbuf_tensor("wob", [128, 2, NE, 512], BF16) as wob,
            nc.sbuf_tensor("s_stage", [128, 4, 512], F32) as s_stage,
            nc.psum_tensor("ps", [128, 4, 512], F32) as ps,
            nc.Block() as b3b,
        ):
            @b3b.sync
            def _(sy):
                if ph_max < 5:
                    return
                sy.wait_ge(st_fuse[0], 16 * 128)
                sy.wait_ge(st_fuse[1], 16 * 128)
                sy.dma_start(out=fuse0[:, :, :],
                             in_=fuse_d[0]).then_inc(ld_fuse[0], 16)

            @b3b.gpsimd
            def _(g):
                if ph_max < 4:
                    return
                for fg in range(NFG):
                    p = fg % 2
                    _wge(g, pe_s, 16 * (fg - 1))
                    g.dma_start(out=wob[:, p], in_=wo_p[fg]).then_inc(ld_wo[p], 16)
                if ph_max >= 5:
                    for b5i in range(2):
                        g.dma_start(out=wdb[:, b5i % 2],
                                    in_=wd_p[0, b5i]).then_inc(ld_wd[b5i % 2], 16)

            @b3b.tensor
            def _(t):
                if ph_max < 4:
                    return
                t.wait_ge(dve_stt, NE)
                for fg in range(NFG):
                    p = fg % 2
                    t.wait_ge(ld_wo[p], 16 * (fg // 2 + 1))
                    for tt in range(NT):
                        idx = 16 * fg + tt
                        r = tt % 4
                        _wge(t, act_s, idx - 3)
                        ins = None
                        for e in range(NE):
                            ins = t.matmul(ps[:, r, :], mix[:, e, 128 * tt:128 * (tt + 1)],
                                           wob[:, p, e, :], start=(e == 0),
                                           stop=(e == NE - 1))
                        ins.then_inc(pe_s, 1)

            @b3b.scalar
            def _(sc):
                if ph_max < 4:
                    return
                for fg in range(NFG):
                    for tt in range(NT):
                        idx = 16 * fg + tt
                        r = tt % 4
                        sc.wait_ge(pe_s, idx + 1)
                        _wge(sc, st_s[r], 16 * (4 * fg + tt // 4))
                        sc.copy(out=s_stage[:, r, :], in_=ps[:, r, :]).then_inc(act_s, 1)
                        sc.dma_start(out=s_d[128 * tt:128 * (tt + 1),
                                             512 * fg:512 * (fg + 1)],
                                     in_=s_stage[:, r, :]).then_inc(st_s[r], 16)

        mix_guard.__exit__(None, None, None)

        fuse1 = ctx.enter_context(nc.sbuf_tensor("fuse1", [128, NM, 512], BF16, side="right"))
        fuse_bufs = (fuse0, fuse1)
        with (
            nc.sbuf_tensor("s_tiles", [128, 2, 4, 512], F32) as s_tiles,
            nc.sbuf_tensor("x_tiles", [128, 2, 4, 512], F32) as x_tiles,
            nc.sbuf_tensor("tmp5", [128, 2, 512], F32) as tmp5,
            nc.sbuf_tensor("gt5", [128, 2, 512], F32) as gt5,
            nc.sbuf_tensor("gs5", [128, 512], F32) as gs5,
            nc.sbuf_tensor("out_st", [128, 2, 512], F32) as out_st,
            nc.psum_tensor("pdn0", [128, 4, 512], F32) as pdn0,
            nc.psum_tensor("pdn1", [128, 4, 512], F32) as pdn1,
            nc.Block() as b5,
        ):
            pdns = (pdn0, pdn1)
            @b5.gpsimd
            def _(g):
                if ph_max < 5:
                    return
                for t4 in range(NT4):
                    for fg in range(NFG):
                        g5 = 4 * t4 + fg
                        for ob in range(NOB):
                            b5i = 16 * g5 + ob
                            if b5i < 2:
                                continue
                            p2 = ob % 2
                            _wge(g, pe_dnblk, b5i - 1)
                            g.dma_start(out=wdb[:, p2],
                                        in_=wd_p[fg, ob]).then_inc(ld_wd[p2], 16)

            @b5.scalar
            def _(sc):
                if ph_max < 5:
                    return
                for r in range(4):
                    sc.wait_ge(st_s[r], 16 * 16)
                for tt2 in range(4):
                    sc.dma_start(out=s_tiles[:, 0, tt2, :],
                                 in_=s_d[128 * tt2:128 * (tt2 + 1), 0:512]).then_inc(ld_sx[0], 16)
                    sc.dma_start(out=x_tiles[:, 0, tt2, :],
                                 in_=x[128 * tt2:128 * (tt2 + 1), 0:512]).then_inc(ld_sx[0], 16)
                for t4 in range(NT4):
                    for fg in range(NFG):
                        g5 = 4 * t4 + fg
                        if g5 + 1 < 16:
                            nt4g = (g5 + 1) // 4; nfg = (g5 + 1) % 4
                            pn = (g5 + 1) % 2
                            _wge(sc, dve_out, 4 * g5)
                            for tt2 in range(4):
                                trow = 128 * (4 * nt4g + tt2)
                                fcol = 512 * nfg
                                sc.dma_start(out=s_tiles[:, pn, tt2, :],
                                             in_=s_d[trow:trow + 128, fcol:fcol + 512]
                                             ).then_inc(ld_sx[pn], 16)
                                sc.dma_start(out=x_tiles[:, pn, tt2, :],
                                             in_=x[trow:trow + 128, fcol:fcol + 512]
                                             ).then_inc(ld_sx[pn], 16)
                        for tt2 in range(4):
                            idx5 = 4 * g5 + tt2
                            sc.wait_ge(dve_tmp, idx5 + 1)
                            _wge(sc, dve_out, idx5 - 1)
                            sc.activation(out=gt5[:, tt2 % 2, :], in_=tmp5[:, tt2 % 2, :],
                                          func=AF.Silu).then_inc(act_g, 1)

            @b5.tensor
            def _(t):
                if ph_max < 5:
                    return
                for t4 in range(NT4):
                    t.wait_ge(ld_fuse[t4 % 2], 16 * (t4 // 2 + 1))
                    fsb = fuse_bufs[t4 % 2]
                    for fg in range(NFG):
                        g5 = 4 * t4 + fg
                        pd = pdns[g5 % 2]
                        _wge(t, dve_out, 4 * (g5 - 1))
                        for ob in range(NOB):
                            b5i = 16 * g5 + ob
                            p2 = ob % 2
                            t.wait_ge(ld_wd[p2], 16 * (b5i // 2 + 1))
                            last_ob = ob == NOB - 1
                            ins = None
                            for oj in range(4):
                                for tt2 in range(4):
                                    ins = t.matmul(
                                        pd[:, tt2, :],
                                        fsb[:, 4 * ob + oj, 128 * tt2:128 * (tt2 + 1)],
                                        wdb[:, p2, oj, :],
                                        start=(ob == 0 and oj == 0),
                                        stop=(last_ob and oj == 3))
                            ins.then_inc(pe_dnblk, 1)

            @b5.vector
            def _(v):
                if ph_max < 5:
                    return
                for t4 in range(NT4):
                    for fg in range(NFG):
                        g5 = 4 * t4 + fg
                        p5 = g5 % 2
                        pd = pdns[g5 % 2]
                        v.wait_ge(pe_dnblk, 16 * (g5 + 1))
                        v.wait_ge(ld_sx[p5], 128 * (g5 // 2 + 1))
                        for tt2 in range(4):
                            idx5 = 4 * g5 + tt2
                            _wge(v, act_g, idx5 - 1)
                            v.tensor_mul(tmp5[:, tt2 % 2, :], pd[:, tt2, :],
                                         sdb[:, 512 * fg:512 * (fg + 1)]).then_inc(dve_tmp, 1)
                            v.wait_ge(act_g, idx5 + 1)
                            v.tensor_mul(gs5[:, :], gt5[:, tt2 % 2, :],
                                         s_tiles[:, p5, tt2, :])
                            _wge(v, st_out[tt2 % 2], 16 * ((idx5 - tt2 % 2) // 2))
                            v.tensor_add(out_st[:, tt2 % 2, :], gs5[:, :],
                                         x_tiles[:, p5, tt2, :]).then_inc(dve_out, 1)

            @b5.sync
            def _(sy):
                if ph_max < 5:
                    return
                sy.dma_start(out=fuse1[:, :, :],
                             in_=fuse_d[1]).then_inc(ld_fuse[1], 16)
                for t4 in range(NT4):
                    for fg in range(NFG):
                        g5 = 4 * t4 + fg
                        for tt2 in range(4):
                            idx5 = 4 * g5 + tt2
                            sy.wait_ge(dve_out, idx5 + 1)
                            trow = 128 * (4 * t4 + tt2)
                            fcol = 512 * fg
                            sy.dma_start(out=out[trow:trow + 128, fcol:fcol + 512],
                                         in_=out_st[:, tt2 % 2, :]).then_inc(st_out[tt2 % 2], 16)
                    if t4 + 2 < NT4:
                        sy.wait_ge(pe_dnblk, 64 * (t4 + 1))
                        sy.dma_start(out=fuse_bufs[t4 % 2][:, :, :],
                                     in_=fuse_d[t4 + 2]).then_inc(ld_fuse[t4 % 2], 16)
                sy.wait_ge(st_out[0], 16 * 32)
                sy.wait_ge(st_out[1], 16 * 32)

    ctx.close()
    return nc


def _pack_inputs(L0_attn_x, L0_attn_Wq, L0_attn_Wk, L0_attn_Wv, L0_attn_Wo,
                 L0_ffn_w_gate, L0_ffn_w_up, L0_ffn_w_down,
                 L0_ffn_s_gate, L0_ffn_s_up, L0_ffn_s_down):
    c = np.ascontiguousarray

    def pack_dxe(W):
        WT = c(np.asarray(W, np.float32).T)
        return c(WT.reshape(ND, 128, NE, 128).transpose(2, 1, 0, 3))

    wq_p = pack_dxe(L0_attn_Wq)
    wk_p = pack_dxe(L0_attn_Wk)
    wv_p = pack_dxe(L0_attn_Wv)
    WoT = c(np.asarray(L0_attn_Wo, np.float32).T)
    wo_p = c(WoT.reshape(NE, 128, NFG, 512).transpose(2, 1, 0, 3))
    wg = np.asarray(L0_ffn_w_gate, np.int8)
    wu = np.asarray(L0_ffn_w_up, np.int8)
    wd = np.asarray(L0_ffn_w_down, np.int8)
    wg_p = c(wg.reshape(ND, 128, NM, 128).transpose(2, 1, 0, 3))
    wu_p = c(wu.reshape(ND, 128, NM, 128).transpose(2, 1, 0, 3))
    wd_p = c(wd.reshape(NOB, 4, 128, NFG, 512).transpose(3, 0, 2, 1, 4))
    sgt = c(np.asarray(L0_ffn_s_gate, np.float32).reshape(NM, 128).T)
    sut = c(np.asarray(L0_ffn_s_up, np.float32).reshape(NM, 128).T)
    sdr = c(np.broadcast_to(np.asarray(L0_ffn_s_down, np.float32).reshape(1, D),
                            (128, D)))
    ident = np.eye(128, dtype=ml_dtypes.bfloat16)
    x_full = np.asarray(L0_attn_x, np.float32)

    in_maps = []
    for core in range(8):
        b, half = core // 2, core % 2
        xc = c(x_full[b, half * T:(half + 1) * T, :])
        selv = np.zeros((8, 1), np.float32)
        if half == 1:
            selv[core - 1, 0] = 1.0
        in_maps.append({
            "x": xc, "wq_p": wq_p, "wk_p": wk_p, "wv_p": wv_p, "wo_p": wo_p,
            "wg_p": wg_p, "wu_p": wu_p, "wd_p": wd_p,
            "sgt": sgt, "sut": sut, "sdr": sdr, "sel": selv, "ident": ident,
        })
    return in_maps


def kernel(L0_attn_x, L0_attn_Wq, L0_attn_Wk, L0_attn_Wv, L0_attn_Wo,
           L0_ffn_w_gate, L0_ffn_w_up, L0_ffn_w_down,
           L0_ffn_s_gate, L0_ffn_s_up, L0_ffn_s_down):
    in_maps = _pack_inputs(L0_attn_x, L0_attn_Wq, L0_attn_Wk, L0_attn_Wv, L0_attn_Wo,
                           L0_ffn_w_gate, L0_ffn_w_up, L0_ffn_w_down,
                           L0_ffn_s_gate, L0_ffn_s_up, L0_ffn_s_down)
    nc = build_nc()
    res = run_bass_kernel_spmd(nc, in_maps, core_ids=list(range(8)))
    out = np.empty((4, 4096, 2048), np.float32)
    for core in range(8):
        b, half = core // 2, core % 2
        out[b, half * T:(half + 1) * T, :] = res.results[core]["out"]
    return out


# version 11
# speedup vs baseline: 1.0493x; 1.0493x over previous
import numpy as np
import ml_dtypes
import concourse.bass as bass
import concourse.mybir as mybir
from concourse.bass_utils import run_bass_kernel_spmd

F32 = mybir.dt.float32
BF16 = mybir.dt.bfloat16
I8 = mybir.dt.int8
AF = mybir.ActivationFunctionType
ALU = mybir.AluOpType
AX = mybir.AxisListType.X

D = 2048
T = 2048
INTER = 8192
ND = 16
NE = 16
NT = 16
NT4 = 4
NM = 64
NFG = 4
NOB = 16
DEN = float(np.float32(np.sqrt(np.float32(2048.0))) / np.float32(4096.0))
EPS = 1e-5


def _wge(eng, sem, v):
    if v > 0:
        eng.wait_ge(sem, v)


def build_nc(ph_max=9, ph0_sub=3, ph1_sub=3, debug=False):
    nc = bass.Bass("TRN2", target_bir_lowering=False, detect_race_conditions=False)

    x = nc.dram_tensor("x", [T, D], F32, kind="ExternalInput").ap()
    wq_p = nc.dram_tensor("wq_p", [NE, 128, ND, 128], F32, kind="ExternalInput").ap()
    wk_p = nc.dram_tensor("wk_p", [NE, 128, ND, 128], F32, kind="ExternalInput").ap()
    wv_p = nc.dram_tensor("wv_p", [NE, 128, ND, 128], F32, kind="ExternalInput").ap()
    wo_p = nc.dram_tensor("wo_p", [NFG, 128, NE, 512], F32, kind="ExternalInput").ap()
    wg_p = nc.dram_tensor("wg_p", [NM, 128, ND, 128], I8, kind="ExternalInput").ap()
    wu_p = nc.dram_tensor("wu_p", [NM, 128, ND, 128], I8, kind="ExternalInput").ap()
    wd_p = nc.dram_tensor("wd_p", [NFG, NOB, 128, 4, 512], I8, kind="ExternalInput").ap()
    sgt = nc.dram_tensor("sgt", [128, NM], F32, kind="ExternalInput").ap()
    sut = nc.dram_tensor("sut", [128, NM], F32, kind="ExternalInput").ap()
    sdr = nc.dram_tensor("sdr", [128, D], F32, kind="ExternalInput").ap()
    sel = nc.dram_tensor("sel", [8, 1], F32, kind="ExternalInput").ap()
    ident = nc.dram_tensor("ident", [128, 128], BF16, kind="ExternalInput").ap()
    out = nc.dram_tensor("out", [T, D], F32, kind="ExternalOutput").ap()

    _skind = "ExternalOutput" if debug else "Internal"
    fuse_d = nc.dram_tensor("fuse_d", [NT4, 128, NM, 512], BF16, kind=_skind).ap()
    s_d = nc.dram_tensor("s_d", [T, D], F32, kind=_skind).ap()
    if debug:
        dbg_xn = nc.dram_tensor("dbg_xn", [128, ND, T], BF16, kind="ExternalOutput").ap()
        dbg_mix = nc.dram_tensor("dbg_mix", [128, NE, T], BF16, kind="ExternalOutput").ap()
        dbg_vhat = nc.dram_tensor("dbg_vhat", [128, NE, T], BF16, kind="ExternalOutput").ap()
        dbg_car = nc.dram_tensor("dbg_car", [128, NE], F32, kind="ExternalOutput").ap()
        dbg_sem = S_dbg = None
    cin = nc.dram_tensor("cin", [128, NE], F32)
    cout = nc.dram_tensor("cout", [8 * 128, NE], F32)

    from contextlib import ExitStack
    ctx = ExitStack()
    S = lambda n: ctx.enter_context(nc.semaphore(n))

    s_small = S("s_small")
    ld_x = (S("ld_x0"), S("ld_x1"))
    ph0_stat = S("ph0_stat"); ph0_sqrt = S("ph0_sqrt"); ph0_norm = S("ph0_norm")
    ph0_sq2 = S("ph0_sq2")
    ph0_tr = S("ph0_tr"); ph0_evac = S("ph0_evac")
    ld_wkv = (S("ld_wkv0"), S("ld_wkv1"))
    pe_ph1 = S("pe_ph1"); act_ph1 = S("act_ph1")
    dve_ph1 = S("dve_ph1"); dve_scan = S("dve_scan"); d_cin = S("d_cin"); cc = S("cc")
    ld_wgu = (S("ld_wgu0"), S("ld_wgu1"), S("ld_wgu2"), S("ld_wgu3"))
    pe_ph4 = S("pe_ph4"); act_ph4 = S("act_ph4")
    dve_ph4u = S("dve_ph4u"); dve_ph4f = S("dve_ph4f")
    st_fuse = (S("st_fuse0"), S("st_fuse1"))
    d_gath = S("d_gath"); pe_car = S("pe_car"); act_car = S("act_car")
    ld_wq = (S("ld_wq0"), S("ld_wq1"))
    pe_q = S("pe_q"); act_q = S("act_q"); dve_stt = S("dve_stt")
    ld_wo = (S("ld_wo0"), S("ld_wo1"))
    pe_s = S("pe_s"); act_s = S("act_s")
    st_s = (S("st_s0"), S("st_s1"), S("st_s2"), S("st_s3"))
    ld_fuse = (S("ld_fuse0"), S("ld_fuse1"))
    ld_wd = (S("ld_wd0"), S("ld_wd1"), S("ld_wd2"), S("ld_wd3"))
    pe_dnblk = S("pe_dnblk")
    dve_tmp = S("dve_tmp"); act_g = S("act_g"); dve_out = S("dve_out")
    ld_sx = (S("ld_sx0"), S("ld_sx1"))
    st_out = (S("st_out0"), S("st_out1"))

    sgt_sb = ctx.enter_context(nc.sbuf_tensor("sgt_sb", [128, NM], F32))
    sut_sb = ctx.enter_context(nc.sbuf_tensor("sut_sb", [128, NM], F32))
    sdb = ctx.enter_context(nc.sbuf_tensor("sdb", [128, D], F32))
    ident_sb = ctx.enter_context(nc.sbuf_tensor("ident_sb", [128, 128], BF16))
    sel_sb = ctx.enter_context(nc.sbuf_tensor("sel_sb", [8, 1], F32))
    h_sb = ctx.enter_context(nc.sbuf_tensor("h_sb", [128, NE], F32))
    carry = ctx.enter_context(nc.sbuf_tensor("carry", [128, NE], F32))

    mix_guard = nc.sbuf_tensor("mix", [128, NE, T], BF16)
    mix = mix_guard.__enter__()
    with (
        nc.sbuf_tensor("xn", [128, ND, T], BF16) as xn,
        nc.sbuf_tensor("wgb", [128, 4, ND, 128], BF16) as wgb,
        nc.sbuf_tensor("wub", [128, 4, ND, 128], BF16) as wub,
    ):
        with (
            nc.sbuf_tensor("xt", [128, 2, D], F32) as xt,
            nc.sbuf_tensor("xnt", [128, 2, D], BF16) as xnt,
            nc.sbuf_tensor("sq", [128, D], BF16) as sq,
            nc.sbuf_tensor("st1", [128, 2], F32) as st1,
            nc.sbuf_tensor("stq", [128, 2], F32) as stq,
            nc.sbuf_tensor("stnm", [128, 2], F32) as stnm,
            nc.sbuf_tensor("stv", [128, 2], F32) as stv,
            nc.sbuf_tensor("stsd", [128, 2], F32) as stsd,
            nc.sbuf_tensor("strs", [128, 2], F32) as strs,
            nc.sbuf_tensor("stw", [128, 2], F32) as stw,
            nc.psum_tensor("ptr0", [128, 4, 128], BF16) as ptr0,
            nc.psum_tensor("ptr1", [128, 4, 128], BF16) as ptr1,
            nc.Block() as b0,
        ):
            ptr = (ptr0, ptr1)

            @b0.sync
            def _(sy):
                sy.dma_start(out=sdb[:, :], in_=sdr[:, :]).then_inc(s_small, 16)
                sy.dma_start(out=sgt_sb[:, :], in_=sgt[:, :]).then_inc(s_small, 16)
                sy.dma_start(out=sut_sb[:, :], in_=sut[:, :]).then_inc(s_small, 16)
                sy.dma_start(out=sel_sb[:, :], in_=sel[:, :]).then_inc(s_small, 16)
                sy.dma_start(out=ident_sb[:, :], in_=ident[:, :]).then_inc(s_small, 16)
                for i in range(NT):
                    _wge(sy, ph0_norm, i - 1)
                    _wge(sy, ph0_sq2, 2 * (i - 1))
                    sy.dma_start(out=xt[:, i % 2, :],
                                 in_=x[128 * i:128 * (i + 1), :]).then_inc(ld_x[i % 2], 16)

            @b0.vector
            def _(v):
                for i in range(NT):
                    p = i % 2
                    v.wait_ge(ld_x[p], 16 * (i // 2 + 1))
                    _wge(v, ph0_sqrt, i - 1)
                    v.wait_ge(ph0_sq2, 2 * (i + 1))
                    v.tensor_scalar_mul(stnm[:, p:p + 1], st1[:, p:p + 1], -1.0 / D)
                    v.tensor_mul(stw[:, p:p + 1], stnm[:, p:p + 1], stnm[:, p:p + 1])
                    v.tensor_scalar_mul(stv[:, p:p + 1], stq[:, p:p + 1], 1.0 / D)
                    v.tensor_sub(stv[:, p:p + 1], stv[:, p:p + 1], stw[:, p:p + 1])
                    v.tensor_scalar_add(stv[:, p:p + 1], stv[:, p:p + 1],
                                        EPS).then_inc(ph0_stat, 1)
                    v.wait_ge(ph0_sqrt, i + 1)
                    v.reciprocal(strs[:, p:p + 1], stsd[:, p:p + 1])
                    if ph0_sub >= 2:
                        _wge(v, ph0_tr, 4 * (i - 1))
                    v.tensor_scalar(xnt[:, p, :], xt[:, p, :], stnm[:, p:p + 1],
                                    strs[:, p:p + 1], ALU.add, ALU.mult).then_inc(ph0_norm, 1)
                    if ph0_sub >= 2 and i > 0:
                        for G in range(4):
                            gidx = 4 * (i - 1) + G
                            v.wait_ge(ph0_tr, gidx + 1)
                            ins = None
                            for j in range(4):
                                ins = v.tensor_copy(xn[:, 4 * G + j, 128 * (i - 1):128 * i],
                                                    ptr[gidx % 2][:, j, :])
                            ins.then_inc(ph0_evac, 1)
                if ph0_sub >= 2:
                    for G in range(4):
                        gidx = 4 * 15 + G
                        v.wait_ge(ph0_tr, gidx + 1)
                        ins = None
                        for j in range(4):
                            ins = v.tensor_copy(xn[:, 4 * G + j, 128 * 15:128 * 16],
                                                ptr[gidx % 2][:, j, :])
                        ins.then_inc(ph0_evac, 1)

            @b0.scalar
            def _(sc):
                sc.preload_act_table(AF.Sqrt)

                def sq2_i(i):
                    p = i % 2
                    sc.wait_ge(ld_x[p], 16 * (i // 2 + 1))
                    _wge(sc, ph0_stat, i - 1)
                    sc.activation(out=sq[:, :], in_=xt[:, p, :], func=AF.Square,
                                  accum_out=stq[:, p:p + 1]).then_inc(ph0_sq2, 1)
                    sc.activation(out=sq[:, :], in_=xt[:, p, :], func=AF.Copy,
                                  accum_out=st1[:, p:p + 1]).then_inc(ph0_sq2, 1)

                def sqrt_i(i):
                    p = i % 2
                    sc.wait_ge(ph0_stat, i + 1)
                    _wge(sc, ph0_norm, i - 1)
                    sc.activation(out=stsd[:, p:p + 1], in_=stv[:, p:p + 1],
                                  func=AF.Sqrt).then_inc(ph0_sqrt, 1)

                sq2_i(0)
                sqrt_i(0)
                for i in range(NT):
                    if i + 1 < NT:
                        sq2_i(i + 1)
                        sqrt_i(i + 1)


            @b0.tensor
            def _(t):
                t.wait_ge(s_small, 80)
                if ph0_sub >= 2:
                    for i in range(NT):
                        p = i % 2
                        t.wait_ge(ph0_norm, i + 1)
                        for G in range(4):
                            gidx = 4 * i + G
                            _wge(t, ph0_evac, gidx - 1)
                            ins = None
                            for j in range(4):
                                n = 4 * G + j
                                ins = t.transpose(ptr[gidx % 2][:, j, :],
                                                  xnt[:, p, 128 * n:128 * (n + 1)],
                                                  ident_sb[:, :])
                            ins.then_inc(ph0_tr, 1)

        with (
            nc.sbuf_tensor("kf32", [128, 2, T], BF16) as kf32,
            nc.sbuf_tensor("kvf32", [128, 2, T], BF16) as kvf32,
            nc.sbuf_tensor("wkb", [128, 2, ND, 128], BF16) as wkb,
            nc.sbuf_tensor("wvb", [128, 2, ND, 128], BF16) as wvb,
            nc.psum_tensor("kpA", [128, 2, 512], F32) as kpA,
            nc.psum_tensor("vpA", [128, 2, 512], F32) as vpA,
            nc.psum_tensor("kpB", [128, 2, 512], F32) as kpB,
            nc.psum_tensor("vpB", [128, 2, 512], F32) as vpB,
            nc.Block() as b1,
        ):
            kp = (kpA, kpB); vp = (vpA, vpB)

            @b1.gpsimd
            def _(g):
                if ph_max < 1:
                    return
                for e in range(NE):
                    p = e % 2
                    _wge(g, pe_ph1, 2 * (e - 1))
                    g.dma_start(out=wkb[:, p], in_=wk_p[e]).then_inc(ld_wkv[p], 16)
                    g.dma_start(out=wvb[:, p], in_=wv_p[e]).then_inc(ld_wkv[p], 16)
                if ph_max >= 2:
                    for m in range(4):
                        g.dma_start(out=wgb[:, m], in_=wg_p[m]).then_inc(ld_wgu[m], 16)
                        g.dma_start(out=wub[:, m], in_=wu_p[m]).then_inc(ld_wgu[m], 16)

            @b1.tensor
            def _(t):
                if ph_max < 1:
                    return
                t.wait_ge(ph0_evac, 64)
                for e in range(NE):
                    p = e % 2
                    t.wait_ge(ld_wkv[p], 32 * (e // 2 + 1))
                    for g2 in range(2):
                        if ph1_sub >= 4:
                            _wge(t, act_ph1, 2 * e + g2)
                            _wge(t, dve_ph1, 2 * e + g2)
                        else:
                            _wge(t, act_ph1, 2 * (e - 1) + g2 + 1)
                            _wge(t, dve_ph1, 2 * (e - 1) + g2 + 1)
                        base = 1024 * g2
                        for d in range(ND):
                            for tt in range(2):
                                t.matmul(kp[g2][:, tt, :], wkb[:, p, d, :],
                                         xn[:, d, base + 512 * tt: base + 512 * (tt + 1)],
                                         start=(d == 0), stop=(d == ND - 1))
                        ins = None
                        for d in range(ND):
                            for tt in range(2):
                                ins = t.matmul(vp[g2][:, tt, :], wvb[:, p, d, :],
                                               xn[:, d, base + 512 * tt: base + 512 * (tt + 1)],
                                               start=(d == 0), stop=(d == ND - 1))
                        ins.then_inc(pe_ph1, 1)

            @b1.scalar
            def _(sc):
                if ph_max < 1:
                    return
                for e in range(NE):
                    p = e % 2
                    _wge(sc, dve_ph1, 2 * (e - 2) + 2)
                    for g2 in range(2):
                        sc.wait_ge(pe_ph1, 2 * e + g2 + 1)
                        base = 1024 * g2
                        sc.copy(out=kf32[:, p, base:base + 512], in_=kp[g2][:, 0, :])
                        sc.copy(out=kf32[:, p, base + 512:base + 1024],
                                in_=kp[g2][:, 1, :]).then_inc(act_ph1, 1)
                sc.wait_ge(dve_scan, NE + 1)
                sc.dma_start(out=cin.ap()[:, :], in_=h_sb[:, :]).then_inc(d_cin, 16)

            @b1.vector
            def _(v):
                if ph_max < 1:
                    return
                for e in range(NE):
                    p = e % 2
                    for g2 in range(2):
                        v.wait_ge(pe_ph1, 2 * e + g2 + 1)
                        v.wait_ge(act_ph1, 2 * e + g2 + 1)
                        base = 1024 * g2
                        if ph1_sub >= 2:
                            v.tensor_mul(kvf32[:, p, base:base + 512],
                                         kf32[:, p, base:base + 512], vp[g2][:, 0, :])
                            v.tensor_mul(kvf32[:, p, base + 512:base + 1024],
                                         kf32[:, p, base + 512:base + 1024],
                                         vp[g2][:, 1, :]).then_inc(dve_ph1, 1)
                        else:
                            v.tensor_copy(kvf32[:, p, base:base + 1024],
                                          kf32[:, p, base:base + 1024]).then_inc(dve_ph1, 1)
                    if ph1_sub >= 3:
                        v.tensor_tensor_scan(mix[:, e, :], kvf32[:, p, :], kvf32[:, p, :],
                                             0.0, ALU.add, ALU.bypass)
                    else:
                        v.tensor_copy(mix[:, e, :], kvf32[:, p, :])
                    v.reduce_sum(h_sb[:, e:e + 1], kvf32[:, p, :],
                                 axis=AX).then_inc(dve_scan, 1)
                v.memset(kvf32[:, 0, 0:8], 0.0).then_inc(dve_scan, 1)

        if debug and ph_max >= 1:
            with nc.semaphore("dbg1") as dbg1, nc.Block() as bdbg1:
                @bdbg1.sync
                def _(sy):
                    sy.wait_ge(ph0_evac, 64)
                    sy.wait_ge(dve_scan, NE)
                    sy.dma_start(out=dbg_xn[:, :, :], in_=xn[:, :, :]).then_inc(dbg1, 16)
                    sy.dma_start(out=dbg_mix[:, :, :], in_=mix[:, :, :]).then_inc(dbg1, 16)
                    sy.wait_ge(dbg1, 32)

        ctx_wqb = nc.sbuf_tensor("wqb", [128, 2, ND, 128], BF16)
        wqb = ctx_wqb.__enter__()
        with (
            nc.sbuf_tensor("gf32", [128, 2, 512], F32) as gf32,
            nc.sbuf_tensor("uf32", [128, 2, 512], F32) as uf32,
            nc.sbuf_tensor("fstage", [128, 2, NT4, 512], BF16) as fstage,
            nc.psum_tensor("pg", [128, 4, 512], F32) as pg,
            nc.psum_tensor("pu", [128, 4, 512], F32) as pu,
            nc.Block() as b4,
        ):
            @b4.gpsimd
            def _(g):
                if ph_max < 2:
                    return
                g.wait_ge(d_cin, 16)
                g.collective_compute(
                    "AllGather", ALU.bypass,
                    replica_groups=[list(range(8))],
                    ins=[cin.ap().opt()], outs=[cout.ap().opt()],
                ).then_inc(cc, 1)
                for m in range(4, NM):
                    p = m % 4
                    _wge(g, pe_ph4, 4 * (m - 3))
                    g.dma_start(out=wgb[:, p], in_=wg_p[m]).then_inc(ld_wgu[p], 16)
                    g.dma_start(out=wub[:, p], in_=wu_p[m]).then_inc(ld_wgu[p], 16)
                if ph_max >= 3:
                    for e in range(2):
                        g.dma_start(out=wqb[:, e], in_=wq_p[e]).then_inc(ld_wq[e], 16)

            @b4.tensor
            def _(t):
                if ph_max < 2:
                    return
                for m in range(NM):
                    p = m % 4
                    t.wait_ge(ld_wgu[p], 32 * (m // 4 + 1))
                    for t4 in range(NT4):
                        idx = 4 * m + t4
                        r = idx % 4
                        _wge(t, act_ph4, idx - 3)
                        _wge(t, dve_ph4u, idx - 3)
                        cols = slice(512 * t4, 512 * (t4 + 1))
                        for d in range(ND):
                            t.matmul(pg[:, r, :], wgb[:, p, d, :], xn[:, d, cols],
                                     start=(d == 0), stop=(d == ND - 1))
                        ins = None
                        for d in range(ND):
                            ins = t.matmul(pu[:, r, :], wub[:, p, d, :], xn[:, d, cols],
                                           start=(d == 0), stop=(d == ND - 1))
                        ins.then_inc(pe_ph4, 1)

            @b4.scalar
            def _(sc):
                if ph_max < 2:
                    return
                sc.preload_act_table(AF.Silu)
                for m in range(NM):
                    for t4 in range(NT4):
                        idx = 4 * m + t4
                        r = idx % 4
                        sc.wait_ge(pe_ph4, idx + 1)
                        _wge(sc, dve_ph4f, idx - 1)
                        sc.activation(out=gf32[:, t4 % 2, :], in_=pg[:, r, :],
                                      func=AF.Silu,
                                      scale=sgt_sb[:, m:m + 1]).then_inc(act_ph4, 1)

            @b4.vector
            def _(v):
                if ph_max < 2:
                    return
                for m in range(NM):
                    p = m % 2
                    for t4 in range(NT4):
                        idx = 4 * m + t4
                        r = idx % 4
                        v.wait_ge(pe_ph4, idx + 1)
                        v.tensor_scalar_mul(uf32[:, t4 % 2, :], pu[:, r, :],
                                            sut_sb[:, m:m + 1]).then_inc(dve_ph4u, 1)
                        v.wait_ge(act_ph4, idx + 1)
                        if t4 == 0:
                            _wge(v, st_fuse[p], 64 * (m // 2))
                        v.tensor_mul(fstage[:, p, t4, :], gf32[:, t4 % 2, :],
                                     uf32[:, t4 % 2, :]).then_inc(dve_ph4f, 1)

            @b4.sync
            def _(sy):
                if ph_max < 2:
                    return
                for m in range(NM):
                    p = m % 2
                    for t4 in range(NT4):
                        idx = 4 * m + t4
                        sy.wait_ge(dve_ph4f, idx + 1)
                        sy.dma_start(out=fuse_d[t4][:, m, :],
                                     in_=fstage[:, p, t4, :]).then_inc(st_fuse[p], 16)

        with (
            nc.sbuf_tensor("qsb", [128, 2, T], BF16) as qsb,
            nc.sbuf_tensor("gath", [8, 128, NE], F32) as gath,
            nc.psum_tensor("pcar", [128, NE], F32) as pcar,
            nc.psum_tensor("pq", [128, 4, 512], F32) as pq,
            nc.Block() as b3,
        ):
            @b3.gpsimd
            def _(g):
                if ph_max < 3:
                    return
                for e in range(2, NE):
                    p = e % 2
                    _wge(g, pe_q, 4 * (e - 1))
                    g.dma_start(out=wqb[:, p], in_=wq_p[e]).then_inc(ld_wq[p], 16)


            @b3.scalar
            def _(sc):
                if ph_max < 3:
                    return
                sc.wait_ge(cc, 1)
                sc.dma_start(out=gath[:, :, :],
                             in_=cout.ap().rearrange("(r p) c -> r p c", p=128)
                             ).then_inc(d_gath, 16)
                sc.wait_ge(pe_car, 1)
                sc.copy(out=carry[:, :], in_=pcar[:, :]).then_inc(act_car, 1)
                for e in range(NE):
                    p = e % 2
                    _wge(sc, dve_stt, e - 1)
                    for t4 in range(NT4):
                        idx = 4 * e + t4
                        sc.wait_ge(pe_q, idx + 1)
                        sc.activation(out=qsb[:, p, 512 * t4:512 * (t4 + 1)],
                                      in_=pq[:, t4, :], func=AF.Copy,
                                      scale=DEN).then_inc(act_q, 1)

            @b3.tensor
            def _(t):
                if ph_max < 3:
                    return
                t.wait_ge(d_gath, 16)
                t.wait_ge(s_small, 80)
                ins = None
                for c in range(NE):
                    ins = t.matmul(pcar[:, c:c + 1], gath[:, :, c], sel_sb[:, :],
                                   start=True, stop=True)
                ins.then_inc(pe_car, 1)
                for e in range(NE):
                    p = e % 2
                    t.wait_ge(ld_wq[p], 16 * (e // 2 + 1))
                    for t4 in range(NT4):
                        idx = 4 * e + t4
                        _wge(t, act_q, idx - 3)
                        cols = slice(512 * t4, 512 * (t4 + 1))
                        ins = None
                        for d in range(ND):
                            ins = t.matmul(pq[:, t4, :], wqb[:, p, d, :], xn[:, d, cols],
                                           start=(d == 0), stop=(d == ND - 1))
                        ins.then_inc(pe_q, 1)

            @b3.vector
            def _(v):
                if ph_max < 3:
                    return
                v.wait_ge(act_car, 1)
                for e in range(NE):
                    p = e % 2
                    v.wait_ge(act_q, 4 * e + 4)
                    v.scalar_tensor_tensor(mix[:, e, :], mix[:, e, :], carry[:, e:e + 1],
                                           qsb[:, p, :], ALU.add,
                                           ALU.mult).then_inc(dve_stt, 1)

        ctx_wqb.__exit__(None, None, None)

    if debug and ph_max >= 3:
        with nc.semaphore("dbg2") as dbg2, nc.Block() as bdbg2:
            @bdbg2.sync
            def _(sy):
                sy.wait_ge(dve_stt, NE)
                sy.wait_ge(act_car, 1)
                sy.dma_start(out=dbg_vhat[:, :, :], in_=mix[:, :, :]).then_inc(dbg2, 16)
                sy.dma_start(out=dbg_car[:, :], in_=carry[:, :]).then_inc(dbg2, 16)
                sy.wait_ge(dbg2, 32)

    fuse0 = ctx.enter_context(nc.sbuf_tensor("fuse0", [128, NM, 512], BF16, side="right"))
    wdb = ctx.enter_context(nc.sbuf_tensor("wdb", [128, 4, 4, 512], BF16, side="right"))
    if True:
        with (
            nc.sbuf_tensor("wob", [128, 2, NE, 512], BF16) as wob,
            nc.sbuf_tensor("s_stage", [128, 4, 512], F32) as s_stage,
            nc.psum_tensor("ps", [128, 4, 512], F32) as ps,
            nc.Block() as b3b,
        ):
            @b3b.sync
            def _(sy):
                if ph_max < 5:
                    return
                sy.wait_ge(st_fuse[0], 16 * 128)
                sy.wait_ge(st_fuse[1], 16 * 128)
                sy.wait_ge(ld_wo[0], 16)
                sy.dma_start(out=fuse0[:, :, :],
                             in_=fuse_d[0]).then_inc(ld_fuse[0], 16)

            @b3b.gpsimd
            def _(g):
                if ph_max < 4:
                    return
                for fg in range(NFG):
                    p = fg % 2
                    _wge(g, pe_s, 16 * (fg - 1))
                    g.dma_start(out=wob[:, p], in_=wo_p[fg]).then_inc(ld_wo[p], 16)
                if ph_max >= 5:
                    for b5i in range(4):
                        g.dma_start(out=wdb[:, b5i],
                                    in_=wd_p[0, b5i]).then_inc(ld_wd[b5i], 16)

            @b3b.tensor
            def _(t):
                if ph_max < 4:
                    return
                t.wait_ge(dve_stt, NE)
                for fg in range(NFG):
                    p = fg % 2
                    t.wait_ge(ld_wo[p], 16 * (fg // 2 + 1))
                    for tt in range(NT):
                        idx = 16 * fg + tt
                        r = tt % 4
                        _wge(t, act_s, idx - 3)
                        ins = None
                        for e in range(NE):
                            ins = t.matmul(ps[:, r, :], mix[:, e, 128 * tt:128 * (tt + 1)],
                                           wob[:, p, e, :], start=(e == 0),
                                           stop=(e == NE - 1))
                        ins.then_inc(pe_s, 1)

            @b3b.scalar
            def _(sc):
                if ph_max < 4:
                    return
                for fg in range(NFG):
                    for tt in range(NT):
                        idx = 16 * fg + tt
                        r = tt % 4
                        sc.wait_ge(pe_s, idx + 1)
                        _wge(sc, st_s[r], 16 * (4 * fg + tt // 4))
                        sc.copy(out=s_stage[:, r, :], in_=ps[:, r, :]).then_inc(act_s, 1)
                        sc.dma_start(out=s_d[128 * tt:128 * (tt + 1),
                                             512 * fg:512 * (fg + 1)],
                                     in_=s_stage[:, r, :]).then_inc(st_s[r], 16)

        mix_guard.__exit__(None, None, None)

        fuse1 = ctx.enter_context(nc.sbuf_tensor("fuse1", [128, NM, 512], BF16, side="right"))
        fuse_bufs = (fuse0, fuse1)
        with (
            nc.sbuf_tensor("s_tiles", [128, 2, 4, 512], F32) as s_tiles,
            nc.sbuf_tensor("x_tiles", [128, 2, 4, 512], F32) as x_tiles,
            nc.sbuf_tensor("tmp5", [128, 2, 512], F32) as tmp5,
            nc.sbuf_tensor("gt5", [128, 2, 512], F32) as gt5,
            nc.sbuf_tensor("gs5", [128, 512], F32) as gs5,
            nc.sbuf_tensor("out_st", [128, 2, 512], F32) as out_st,
            nc.psum_tensor("pdn0", [128, 4, 512], F32) as pdn0,
            nc.psum_tensor("pdn1", [128, 4, 512], F32) as pdn1,
            nc.Block() as b5,
        ):
            pdns = (pdn0, pdn1)
            @b5.gpsimd
            def _(g):
                if ph_max < 5:
                    return
                for t4 in range(NT4):
                    for fg in range(NFG):
                        g5 = 4 * t4 + fg
                        for ob in range(NOB):
                            b5i = 16 * g5 + ob
                            if b5i < 4:
                                continue
                            p2 = b5i % 4
                            _wge(g, pe_dnblk, b5i - 3)
                            g.dma_start(out=wdb[:, p2],
                                        in_=wd_p[fg, ob]).then_inc(ld_wd[p2], 16)

            @b5.scalar
            def _(sc):
                if ph_max < 5:
                    return
                for r in range(4):
                    sc.wait_ge(st_s[r], 16 * 16)
                for tt2 in range(4):
                    sc.dma_start(out=s_tiles[:, 0, tt2, :],
                                 in_=s_d[128 * tt2:128 * (tt2 + 1), 0:512]).then_inc(ld_sx[0], 16)
                    sc.dma_start(out=x_tiles[:, 0, tt2, :],
                                 in_=x[128 * tt2:128 * (tt2 + 1), 0:512]).then_inc(ld_sx[0], 16)
                for t4 in range(NT4):
                    for fg in range(NFG):
                        g5 = 4 * t4 + fg
                        if g5 + 1 < 16:
                            nt4g = (g5 + 1) // 4; nfg = (g5 + 1) % 4
                            pn = (g5 + 1) % 2
                            _wge(sc, dve_out, 4 * g5)
                            for tt2 in range(4):
                                trow = 128 * (4 * nt4g + tt2)
                                fcol = 512 * nfg
                                sc.dma_start(out=s_tiles[:, pn, tt2, :],
                                             in_=s_d[trow:trow + 128, fcol:fcol + 512]
                                             ).then_inc(ld_sx[pn], 16)
                                sc.dma_start(out=x_tiles[:, pn, tt2, :],
                                             in_=x[trow:trow + 128, fcol:fcol + 512]
                                             ).then_inc(ld_sx[pn], 16)
                        for tt2 in range(4):
                            idx5 = 4 * g5 + tt2
                            sc.wait_ge(dve_tmp, idx5 + 1)
                            _wge(sc, dve_out, idx5 - 1)
                            sc.activation(out=gt5[:, tt2 % 2, :], in_=tmp5[:, tt2 % 2, :],
                                          func=AF.Silu).then_inc(act_g, 1)

            @b5.tensor
            def _(t):
                if ph_max < 5:
                    return
                for t4 in range(NT4):
                    t.wait_ge(ld_fuse[t4 % 2], 16 * (t4 // 2 + 1))
                    fsb = fuse_bufs[t4 % 2]
                    for fg in range(NFG):
                        g5 = 4 * t4 + fg
                        pd = pdns[g5 % 2]
                        _wge(t, dve_out, 4 * (g5 - 1))
                        for ob in range(NOB):
                            b5i = 16 * g5 + ob
                            p2 = b5i % 4
                            t.wait_ge(ld_wd[p2], 16 * (b5i // 4 + 1))
                            last_ob = ob == NOB - 1
                            ins = None
                            for oj in range(4):
                                for tt2 in range(4):
                                    ins = t.matmul(
                                        pd[:, tt2, :],
                                        fsb[:, 4 * ob + oj, 128 * tt2:128 * (tt2 + 1)],
                                        wdb[:, p2, oj, :],
                                        start=(ob == 0 and oj == 0),
                                        stop=(last_ob and oj == 3))
                            ins.then_inc(pe_dnblk, 1)

            @b5.vector
            def _(v):
                if ph_max < 5:
                    return
                for t4 in range(NT4):
                    for fg in range(NFG):
                        g5 = 4 * t4 + fg
                        p5 = g5 % 2
                        pd = pdns[g5 % 2]
                        v.wait_ge(pe_dnblk, 16 * (g5 + 1))
                        v.wait_ge(ld_sx[p5], 128 * (g5 // 2 + 1))
                        for tt2 in range(4):
                            idx5 = 4 * g5 + tt2
                            _wge(v, act_g, idx5 - 1)
                            v.tensor_mul(tmp5[:, tt2 % 2, :], pd[:, tt2, :],
                                         sdb[:, 512 * fg:512 * (fg + 1)]).then_inc(dve_tmp, 1)
                            v.wait_ge(act_g, idx5 + 1)
                            v.tensor_mul(gs5[:, :], gt5[:, tt2 % 2, :],
                                         s_tiles[:, p5, tt2, :])
                            _wge(v, st_out[tt2 % 2], 16 * ((idx5 - tt2 % 2) // 2))
                            v.tensor_add(out_st[:, tt2 % 2, :], gs5[:, :],
                                         x_tiles[:, p5, tt2, :]).then_inc(dve_out, 1)

            @b5.sync
            def _(sy):
                if ph_max < 5:
                    return
                sy.dma_start(out=fuse1[:, :, :],
                             in_=fuse_d[1]).then_inc(ld_fuse[1], 16)
                for t4 in range(NT4):
                    for fg in range(NFG):
                        g5 = 4 * t4 + fg
                        for tt2 in range(4):
                            idx5 = 4 * g5 + tt2
                            sy.wait_ge(dve_out, idx5 + 1)
                            trow = 128 * (4 * t4 + tt2)
                            fcol = 512 * fg
                            sy.dma_start(out=out[trow:trow + 128, fcol:fcol + 512],
                                         in_=out_st[:, tt2 % 2, :]).then_inc(st_out[tt2 % 2], 16)
                    if t4 + 2 < NT4:
                        sy.wait_ge(pe_dnblk, 64 * (t4 + 1))
                        sy.dma_start(out=fuse_bufs[t4 % 2][:, :, :],
                                     in_=fuse_d[t4 + 2]).then_inc(ld_fuse[t4 % 2], 16)
                sy.wait_ge(st_out[0], 16 * 32)
                sy.wait_ge(st_out[1], 16 * 32)

    ctx.close()
    return nc


def _pack_inputs(L0_attn_x, L0_attn_Wq, L0_attn_Wk, L0_attn_Wv, L0_attn_Wo,
                 L0_ffn_w_gate, L0_ffn_w_up, L0_ffn_w_down,
                 L0_ffn_s_gate, L0_ffn_s_up, L0_ffn_s_down):
    c = np.ascontiguousarray

    def pack_dxe(W):
        WT = c(np.asarray(W, np.float32).T)
        return c(WT.reshape(ND, 128, NE, 128).transpose(2, 1, 0, 3))

    wq_p = pack_dxe(L0_attn_Wq)
    wk_p = pack_dxe(L0_attn_Wk)
    wv_p = pack_dxe(L0_attn_Wv)
    WoT = c(np.asarray(L0_attn_Wo, np.float32).T)
    wo_p = c(WoT.reshape(NE, 128, NFG, 512).transpose(2, 1, 0, 3))
    wg = np.asarray(L0_ffn_w_gate, np.int8)
    wu = np.asarray(L0_ffn_w_up, np.int8)
    wd = np.asarray(L0_ffn_w_down, np.int8)
    wg_p = c(wg.reshape(ND, 128, NM, 128).transpose(2, 1, 0, 3))
    wu_p = c(wu.reshape(ND, 128, NM, 128).transpose(2, 1, 0, 3))
    wd_p = c(wd.reshape(NOB, 4, 128, NFG, 512).transpose(3, 0, 2, 1, 4))
    sgt = c(np.asarray(L0_ffn_s_gate, np.float32).reshape(NM, 128).T)
    sut = c(np.asarray(L0_ffn_s_up, np.float32).reshape(NM, 128).T)
    sdr = c(np.broadcast_to(np.asarray(L0_ffn_s_down, np.float32).reshape(1, D),
                            (128, D)))
    ident = np.eye(128, dtype=ml_dtypes.bfloat16)
    x_full = np.asarray(L0_attn_x, np.float32)

    in_maps = []
    for core in range(8):
        b, half = core // 2, core % 2
        xc = c(x_full[b, half * T:(half + 1) * T, :])
        selv = np.zeros((8, 1), np.float32)
        if half == 1:
            selv[core - 1, 0] = 1.0
        in_maps.append({
            "x": xc, "wq_p": wq_p, "wk_p": wk_p, "wv_p": wv_p, "wo_p": wo_p,
            "wg_p": wg_p, "wu_p": wu_p, "wd_p": wd_p,
            "sgt": sgt, "sut": sut, "sdr": sdr, "sel": selv, "ident": ident,
        })
    return in_maps


def kernel(L0_attn_x, L0_attn_Wq, L0_attn_Wk, L0_attn_Wv, L0_attn_Wo,
           L0_ffn_w_gate, L0_ffn_w_up, L0_ffn_w_down,
           L0_ffn_s_gate, L0_ffn_s_up, L0_ffn_s_down):
    in_maps = _pack_inputs(L0_attn_x, L0_attn_Wq, L0_attn_Wk, L0_attn_Wv, L0_attn_Wo,
                           L0_ffn_w_gate, L0_ffn_w_up, L0_ffn_w_down,
                           L0_ffn_s_gate, L0_ffn_s_up, L0_ffn_s_down)
    nc = build_nc()
    res = run_bass_kernel_spmd(nc, in_maps, core_ids=list(range(8)))
    out = np.empty((4, 4096, 2048), np.float32)
    for core in range(8):
        b, half = core // 2, core % 2
        out[b, half * T:(half + 1) * T, :] = res.results[core]["out"]
    return out


# version 12
# speedup vs baseline: 1.0516x; 1.0022x over previous
import numpy as np
import ml_dtypes
import concourse.bass as bass
import concourse.mybir as mybir
from concourse.bass_utils import run_bass_kernel_spmd

F32 = mybir.dt.float32
BF16 = mybir.dt.bfloat16
I8 = mybir.dt.int8
AF = mybir.ActivationFunctionType
ALU = mybir.AluOpType
AX = mybir.AxisListType.X

D = 2048
T = 2048
INTER = 8192
ND = 16
NE = 16
NT = 16
NT4 = 4
NM = 64
NFG = 4
NOB = 16
DEN = float(np.float32(np.sqrt(np.float32(2048.0))) / np.float32(4096.0))
EPS = 1e-5


def _wge(eng, sem, v):
    if v > 0:
        eng.wait_ge(sem, v)


def build_nc(ph_max=9, ph0_sub=3, ph1_sub=3, debug=False):
    nc = bass.Bass("TRN2", target_bir_lowering=False, detect_race_conditions=False)

    x = nc.dram_tensor("x", [T, D], F32, kind="ExternalInput").ap()
    wq_p = nc.dram_tensor("wq_p", [NE, 128, ND, 128], F32, kind="ExternalInput").ap()
    wk_p = nc.dram_tensor("wk_p", [NE, 128, ND, 128], F32, kind="ExternalInput").ap()
    wv_p = nc.dram_tensor("wv_p", [NE, 128, ND, 128], F32, kind="ExternalInput").ap()
    wo_p = nc.dram_tensor("wo_p", [NFG, 128, NE, 512], F32, kind="ExternalInput").ap()
    wg_p = nc.dram_tensor("wg_p", [NM, 128, ND, 128], I8, kind="ExternalInput").ap()
    wu_p = nc.dram_tensor("wu_p", [NM, 128, ND, 128], I8, kind="ExternalInput").ap()
    wd_p = nc.dram_tensor("wd_p", [NFG, NOB, 128, 4, 512], I8, kind="ExternalInput").ap()
    sgt = nc.dram_tensor("sgt", [128, NM], F32, kind="ExternalInput").ap()
    sut = nc.dram_tensor("sut", [128, NM], F32, kind="ExternalInput").ap()
    sdr = nc.dram_tensor("sdr", [128, D], F32, kind="ExternalInput").ap()
    sel = nc.dram_tensor("sel", [8, 1], F32, kind="ExternalInput").ap()
    ident = nc.dram_tensor("ident", [128, 128], BF16, kind="ExternalInput").ap()
    out = nc.dram_tensor("out", [T, D], F32, kind="ExternalOutput").ap()

    _skind = "ExternalOutput" if debug else "Internal"
    fuse_d = nc.dram_tensor("fuse_d", [NT4, 128, NM, 512], BF16, kind=_skind).ap()
    s_d = nc.dram_tensor("s_d", [T, D], F32, kind=_skind).ap()
    if debug:
        dbg_xn = nc.dram_tensor("dbg_xn", [128, ND, T], BF16, kind="ExternalOutput").ap()
        dbg_mix = nc.dram_tensor("dbg_mix", [128, NE, T], BF16, kind="ExternalOutput").ap()
        dbg_vhat = nc.dram_tensor("dbg_vhat", [128, NE, T], BF16, kind="ExternalOutput").ap()
        dbg_car = nc.dram_tensor("dbg_car", [128, NE], F32, kind="ExternalOutput").ap()
        dbg_sem = S_dbg = None
    cin = nc.dram_tensor("cin", [128, NE], F32)
    cout = nc.dram_tensor("cout", [8 * 128, NE], F32)

    from contextlib import ExitStack
    ctx = ExitStack()
    S = lambda n: ctx.enter_context(nc.semaphore(n))

    s_small = S("s_small")
    ld_x = (S("ld_x0"), S("ld_x1"))
    ph0_stat = S("ph0_stat"); ph0_sqrt = S("ph0_sqrt"); ph0_norm = S("ph0_norm")
    ph0_sq2 = S("ph0_sq2")
    ph0_tr = S("ph0_tr"); ph0_evac = S("ph0_evac")
    ld_wkv = (S("ld_wkv0"), S("ld_wkv1"))
    pe_ph1 = S("pe_ph1"); act_ph1 = S("act_ph1")
    dve_ph1 = S("dve_ph1"); dve_scan = S("dve_scan"); d_cin = S("d_cin"); cc = S("cc")
    ld_wgu = (S("ld_wgu0"), S("ld_wgu1"), S("ld_wgu2"), S("ld_wgu3"))
    pe_ph4 = S("pe_ph4"); act_ph4 = S("act_ph4")
    dve_ph4u = S("dve_ph4u"); dve_ph4f = S("dve_ph4f")
    st_fuse = (S("st_fuse0"), S("st_fuse1"))
    d_gath = S("d_gath"); pe_car = S("pe_car"); act_car = S("act_car")
    ld_wq = (S("ld_wq0"), S("ld_wq1"))
    pe_q = S("pe_q"); act_q = S("act_q"); dve_stt = S("dve_stt")
    ld_wo = (S("ld_wo0"), S("ld_wo1"))
    pe_s = S("pe_s"); act_s = S("act_s")
    st_s = (S("st_s0"), S("st_s1"), S("st_s2"), S("st_s3"))
    ld_fuse = (S("ld_fuse0"), S("ld_fuse1"))
    ld_wd = (S("ld_wd0"), S("ld_wd1"), S("ld_wd2"), S("ld_wd3"))
    pe_dnblk = S("pe_dnblk")
    dve_tmp = S("dve_tmp"); act_g = S("act_g"); dve_out = S("dve_out")
    ld_sx = (S("ld_sx0"), S("ld_sx1"))
    st_out = (S("st_out0"), S("st_out1"))

    sgt_sb = ctx.enter_context(nc.sbuf_tensor("sgt_sb", [128, NM], F32))
    sut_sb = ctx.enter_context(nc.sbuf_tensor("sut_sb", [128, NM], F32))
    sdb = ctx.enter_context(nc.sbuf_tensor("sdb", [128, D], F32))
    ident_sb = ctx.enter_context(nc.sbuf_tensor("ident_sb", [128, 128], BF16))
    sel_sb = ctx.enter_context(nc.sbuf_tensor("sel_sb", [8, 1], F32))
    h_sb = ctx.enter_context(nc.sbuf_tensor("h_sb", [128, NE], F32))
    carry = ctx.enter_context(nc.sbuf_tensor("carry", [128, NE], F32))

    mix_guard = nc.sbuf_tensor("mix", [128, NE, T], BF16)
    mix = mix_guard.__enter__()
    with (
        nc.sbuf_tensor("xn", [128, ND, T], BF16) as xn,
        nc.sbuf_tensor("wgb", [128, 4, ND, 128], BF16) as wgb,
        nc.sbuf_tensor("wub", [128, 4, ND, 128], BF16) as wub,
    ):
        with (
            nc.sbuf_tensor("xt", [128, 2, D], F32) as xt,
            nc.sbuf_tensor("xnt", [128, 2, D], BF16) as xnt,
            nc.sbuf_tensor("sq", [128, D], BF16) as sq,
            nc.sbuf_tensor("st1", [128, 2], F32) as st1,
            nc.sbuf_tensor("stq", [128, 2], F32) as stq,
            nc.sbuf_tensor("stnm", [128, 2], F32) as stnm,
            nc.sbuf_tensor("stv", [128, 2], F32) as stv,
            nc.sbuf_tensor("stsd", [128, 2], F32) as stsd,
            nc.sbuf_tensor("strs", [128, 2], F32) as strs,
            nc.sbuf_tensor("stw", [128, 2], F32) as stw,
            nc.psum_tensor("ptr0", [128, 4, 128], BF16) as ptr0,
            nc.psum_tensor("ptr1", [128, 4, 128], BF16) as ptr1,
            nc.Block() as b0,
        ):
            ptr = (ptr0, ptr1)

            @b0.sync
            def _(sy):
                sy.dma_start(out=sdb[:, :], in_=sdr[:, :]).then_inc(s_small, 16)
                sy.dma_start(out=sgt_sb[:, :], in_=sgt[:, :]).then_inc(s_small, 16)
                sy.dma_start(out=sut_sb[:, :], in_=sut[:, :]).then_inc(s_small, 16)
                sy.dma_start(out=sel_sb[:, :], in_=sel[:, :]).then_inc(s_small, 16)
                sy.dma_start(out=ident_sb[:, :], in_=ident[:, :]).then_inc(s_small, 16)
                for i in range(NT):
                    _wge(sy, ph0_norm, i - 1)
                    _wge(sy, ph0_sq2, 2 * (i - 1))
                    sy.dma_start(out=xt[:, i % 2, :],
                                 in_=x[128 * i:128 * (i + 1), :]).then_inc(ld_x[i % 2], 16)

            @b0.vector
            def _(v):
                for i in range(NT):
                    p = i % 2
                    v.wait_ge(ld_x[p], 16 * (i // 2 + 1))
                    _wge(v, ph0_sqrt, i - 1)
                    v.wait_ge(ph0_sq2, 2 * (i + 1))
                    v.tensor_scalar_mul(stnm[:, p:p + 1], st1[:, p:p + 1], -1.0 / D)
                    v.tensor_mul(stw[:, p:p + 1], stnm[:, p:p + 1], stnm[:, p:p + 1])
                    v.tensor_scalar_mul(stv[:, p:p + 1], stq[:, p:p + 1], 1.0 / D)
                    v.tensor_sub(stv[:, p:p + 1], stv[:, p:p + 1], stw[:, p:p + 1])
                    v.tensor_scalar_add(stv[:, p:p + 1], stv[:, p:p + 1],
                                        EPS).then_inc(ph0_stat, 1)
                    v.wait_ge(ph0_sqrt, i + 1)
                    v.reciprocal(strs[:, p:p + 1], stsd[:, p:p + 1])
                    if ph0_sub >= 2:
                        _wge(v, ph0_tr, 4 * (i - 1))
                    v.tensor_scalar(xnt[:, p, :], xt[:, p, :], stnm[:, p:p + 1],
                                    strs[:, p:p + 1], ALU.add, ALU.mult).then_inc(ph0_norm, 1)
                    if ph0_sub >= 2 and i > 0:
                        for G in range(4):
                            gidx = 4 * (i - 1) + G
                            v.wait_ge(ph0_tr, gidx + 1)
                            ins = None
                            for j in range(4):
                                ins = v.tensor_copy(xn[:, 4 * G + j, 128 * (i - 1):128 * i],
                                                    ptr[gidx % 2][:, j, :])
                            ins.then_inc(ph0_evac, 1)
                if ph0_sub >= 2:
                    for G in range(4):
                        gidx = 4 * 15 + G
                        v.wait_ge(ph0_tr, gidx + 1)
                        ins = None
                        for j in range(4):
                            ins = v.tensor_copy(xn[:, 4 * G + j, 128 * 15:128 * 16],
                                                ptr[gidx % 2][:, j, :])
                        ins.then_inc(ph0_evac, 1)

            @b0.scalar
            def _(sc):
                sc.preload_act_table(AF.Sqrt)

                def sq2_i(i):
                    p = i % 2
                    sc.wait_ge(ld_x[p], 16 * (i // 2 + 1))
                    _wge(sc, ph0_stat, i - 1)
                    sc.activation(out=sq[:, :], in_=xt[:, p, :], func=AF.Square,
                                  accum_out=stq[:, p:p + 1]).then_inc(ph0_sq2, 1)
                    sc.activation(out=sq[:, :], in_=xt[:, p, :], func=AF.Copy,
                                  accum_out=st1[:, p:p + 1]).then_inc(ph0_sq2, 1)

                def sqrt_i(i):
                    p = i % 2
                    sc.wait_ge(ph0_stat, i + 1)
                    _wge(sc, ph0_norm, i - 1)
                    sc.activation(out=stsd[:, p:p + 1], in_=stv[:, p:p + 1],
                                  func=AF.Sqrt).then_inc(ph0_sqrt, 1)

                sq2_i(0)
                sqrt_i(0)
                for i in range(NT):
                    if i + 1 < NT:
                        sq2_i(i + 1)
                        sqrt_i(i + 1)


            @b0.tensor
            def _(t):
                t.wait_ge(s_small, 80)
                if ph0_sub >= 2:
                    for i in range(NT):
                        p = i % 2
                        t.wait_ge(ph0_norm, i + 1)
                        for G in range(4):
                            gidx = 4 * i + G
                            _wge(t, ph0_evac, gidx - 1)
                            ins = None
                            for j in range(4):
                                n = 4 * G + j
                                ins = t.transpose(ptr[gidx % 2][:, j, :],
                                                  xnt[:, p, 128 * n:128 * (n + 1)],
                                                  ident_sb[:, :])
                            ins.then_inc(ph0_tr, 1)

        with (
            nc.sbuf_tensor("kf32", [128, 2, T], BF16) as kf32,
            nc.sbuf_tensor("kvf32", [128, 2, T], BF16) as kvf32,
            nc.sbuf_tensor("wkb", [128, 2, ND, 128], BF16) as wkb,
            nc.sbuf_tensor("wvb", [128, 2, ND, 128], BF16) as wvb,
            nc.psum_tensor("kpA", [128, 2, 512], F32) as kpA,
            nc.psum_tensor("vpA", [128, 2, 512], F32) as vpA,
            nc.psum_tensor("kpB", [128, 2, 512], F32) as kpB,
            nc.psum_tensor("vpB", [128, 2, 512], F32) as vpB,
            nc.Block() as b1,
        ):
            kp = (kpA, kpB); vp = (vpA, vpB)

            @b1.gpsimd
            def _(g):
                if ph_max < 1:
                    return
                for e in range(NE):
                    p = e % 2
                    _wge(g, pe_ph1, 2 * (e - 1))
                    g.dma_start(out=wkb[:, p], in_=wk_p[e]).then_inc(ld_wkv[p], 16)
                    g.dma_start(out=wvb[:, p], in_=wv_p[e]).then_inc(ld_wkv[p], 16)
                if ph_max >= 2:
                    for m in range(4):
                        g.dma_start(out=wgb[:, m], in_=wg_p[m]).then_inc(ld_wgu[m], 16)
                        g.dma_start(out=wub[:, m], in_=wu_p[m]).then_inc(ld_wgu[m], 16)

            @b1.tensor
            def _(t):
                if ph_max < 1:
                    return
                t.wait_ge(ph0_evac, 64)
                for e in range(NE):
                    p = e % 2
                    t.wait_ge(ld_wkv[p], 32 * (e // 2 + 1))
                    for g2 in range(2):
                        if ph1_sub >= 4:
                            _wge(t, act_ph1, 2 * e + g2)
                            _wge(t, dve_ph1, 2 * e + g2)
                        else:
                            _wge(t, act_ph1, 2 * (e - 1) + g2 + 1)
                            _wge(t, dve_ph1, 2 * (e - 1) + g2 + 1)
                        base = 1024 * g2
                        for d in range(ND):
                            for tt in range(2):
                                t.matmul(kp[g2][:, tt, :], wkb[:, p, d, :],
                                         xn[:, d, base + 512 * tt: base + 512 * (tt + 1)],
                                         start=(d == 0), stop=(d == ND - 1))
                        ins = None
                        for d in range(ND):
                            for tt in range(2):
                                ins = t.matmul(vp[g2][:, tt, :], wvb[:, p, d, :],
                                               xn[:, d, base + 512 * tt: base + 512 * (tt + 1)],
                                               start=(d == 0), stop=(d == ND - 1))
                        ins.then_inc(pe_ph1, 1)

            @b1.scalar
            def _(sc):
                if ph_max < 1:
                    return
                for e in range(NE):
                    p = e % 2
                    _wge(sc, dve_ph1, 2 * (e - 2) + 2)
                    for g2 in range(2):
                        sc.wait_ge(pe_ph1, 2 * e + g2 + 1)
                        base = 1024 * g2
                        sc.copy(out=kf32[:, p, base:base + 512], in_=kp[g2][:, 0, :])
                        sc.copy(out=kf32[:, p, base + 512:base + 1024],
                                in_=kp[g2][:, 1, :]).then_inc(act_ph1, 1)
                sc.wait_ge(dve_scan, NE + 1)
                sc.dma_start(out=cin.ap()[:, :], in_=h_sb[:, :]).then_inc(d_cin, 16)

            @b1.vector
            def _(v):
                if ph_max < 1:
                    return
                for e in range(NE):
                    p = e % 2
                    for g2 in range(2):
                        v.wait_ge(pe_ph1, 2 * e + g2 + 1)
                        v.wait_ge(act_ph1, 2 * e + g2 + 1)
                        base = 1024 * g2
                        if ph1_sub >= 2:
                            v.tensor_mul(kvf32[:, p, base:base + 512],
                                         kf32[:, p, base:base + 512], vp[g2][:, 0, :])
                            v.tensor_mul(kvf32[:, p, base + 512:base + 1024],
                                         kf32[:, p, base + 512:base + 1024],
                                         vp[g2][:, 1, :]).then_inc(dve_ph1, 1)
                        else:
                            v.tensor_copy(kvf32[:, p, base:base + 1024],
                                          kf32[:, p, base:base + 1024]).then_inc(dve_ph1, 1)
                    if ph1_sub >= 3:
                        v.tensor_tensor_scan(mix[:, e, :], kvf32[:, p, :], kvf32[:, p, :],
                                             0.0, ALU.add, ALU.bypass)
                    else:
                        v.tensor_copy(mix[:, e, :], kvf32[:, p, :])
                    v.reduce_sum(h_sb[:, e:e + 1], kvf32[:, p, :],
                                 axis=AX).then_inc(dve_scan, 1)
                v.memset(kvf32[:, 0, 0:8], 0.0).then_inc(dve_scan, 1)

        if debug and ph_max >= 1:
            with nc.semaphore("dbg1") as dbg1, nc.Block() as bdbg1:
                @bdbg1.sync
                def _(sy):
                    sy.wait_ge(ph0_evac, 64)
                    sy.wait_ge(dve_scan, NE)
                    sy.dma_start(out=dbg_xn[:, :, :], in_=xn[:, :, :]).then_inc(dbg1, 16)
                    sy.dma_start(out=dbg_mix[:, :, :], in_=mix[:, :, :]).then_inc(dbg1, 16)
                    sy.wait_ge(dbg1, 32)

        ctx_wqb = nc.sbuf_tensor("wqb", [128, 2, ND, 128], BF16)
        wqb = ctx_wqb.__enter__()
        with (
            nc.sbuf_tensor("gf32", [128, 2, 512], F32) as gf32,
            nc.sbuf_tensor("uf32", [128, 2, 512], F32) as uf32,
            nc.sbuf_tensor("fstage", [128, 2, NT4, 512], BF16) as fstage,
            nc.psum_tensor("pg", [128, 4, 512], F32) as pg,
            nc.psum_tensor("pu", [128, 4, 512], F32) as pu,
            nc.Block() as b4,
        ):
            @b4.gpsimd
            def _(g):
                if ph_max < 2:
                    return
                g.wait_ge(d_cin, 16)
                g.collective_compute(
                    "AllGather", ALU.bypass,
                    replica_groups=[list(range(8))],
                    ins=[cin.ap().opt()], outs=[cout.ap().opt()],
                ).then_inc(cc, 1)
                for m in range(4, NM):
                    p = m % 4
                    _wge(g, pe_ph4, 4 * (m - 3))
                    g.dma_start(out=wgb[:, p], in_=wg_p[m]).then_inc(ld_wgu[p], 16)
                    g.dma_start(out=wub[:, p], in_=wu_p[m]).then_inc(ld_wgu[p], 16)
                if ph_max >= 3:
                    for e in range(2):
                        g.dma_start(out=wqb[:, e], in_=wq_p[e]).then_inc(ld_wq[e], 16)

            @b4.tensor
            def _(t):
                if ph_max < 2:
                    return
                for m in range(NM):
                    p = m % 4
                    t.wait_ge(ld_wgu[p], 32 * (m // 4 + 1))
                    for t4 in range(NT4):
                        idx = 4 * m + t4
                        r = idx % 4
                        _wge(t, act_ph4, idx - 3)
                        _wge(t, dve_ph4u, idx - 3)
                        cols = slice(512 * t4, 512 * (t4 + 1))
                        for d in range(ND):
                            t.matmul(pg[:, r, :], wgb[:, p, d, :], xn[:, d, cols],
                                     start=(d == 0), stop=(d == ND - 1))
                        ins = None
                        for d in range(ND):
                            ins = t.matmul(pu[:, r, :], wub[:, p, d, :], xn[:, d, cols],
                                           start=(d == 0), stop=(d == ND - 1))
                        ins.then_inc(pe_ph4, 1)

            @b4.scalar
            def _(sc):
                if ph_max < 2:
                    return
                sc.preload_act_table(AF.Silu)
                for m in range(NM):
                    for t4 in range(NT4):
                        idx = 4 * m + t4
                        r = idx % 4
                        sc.wait_ge(pe_ph4, idx + 1)
                        _wge(sc, dve_ph4f, idx - 1)
                        sc.activation(out=gf32[:, t4 % 2, :], in_=pg[:, r, :],
                                      func=AF.Silu,
                                      scale=sgt_sb[:, m:m + 1]).then_inc(act_ph4, 1)

            @b4.vector
            def _(v):
                if ph_max < 2:
                    return
                for m in range(NM):
                    p = m % 2
                    for t4 in range(NT4):
                        idx = 4 * m + t4
                        r = idx % 4
                        v.wait_ge(pe_ph4, idx + 1)
                        v.tensor_scalar_mul(uf32[:, t4 % 2, :], pu[:, r, :],
                                            sut_sb[:, m:m + 1]).then_inc(dve_ph4u, 1)
                        v.wait_ge(act_ph4, idx + 1)
                        if t4 == 0:
                            _wge(v, st_fuse[p], 64 * (m // 2))
                        v.tensor_mul(fstage[:, p, t4, :], gf32[:, t4 % 2, :],
                                     uf32[:, t4 % 2, :]).then_inc(dve_ph4f, 1)

            @b4.sync
            def _(sy):
                if ph_max < 2:
                    return
                for m in range(NM):
                    p = m % 2
                    for t4 in range(NT4):
                        idx = 4 * m + t4
                        sy.wait_ge(dve_ph4f, idx + 1)
                        sy.dma_start(out=fuse_d[t4][:, m, :],
                                     in_=fstage[:, p, t4, :]).then_inc(st_fuse[p], 16)

        with (
            nc.sbuf_tensor("qsb", [128, 2, T], BF16) as qsb,
            nc.sbuf_tensor("gath", [8, 128, NE], F32) as gath,
            nc.psum_tensor("pcar", [128, NE], F32) as pcar,
            nc.psum_tensor("pq", [128, 4, 512], F32) as pq,
            nc.Block() as b3,
        ):
            @b3.gpsimd
            def _(g):
                if ph_max < 3:
                    return
                for e in range(2, NE):
                    p = e % 2
                    _wge(g, pe_q, 4 * (e - 1))
                    g.dma_start(out=wqb[:, p], in_=wq_p[e]).then_inc(ld_wq[p], 16)


            @b3.scalar
            def _(sc):
                if ph_max < 3:
                    return
                sc.wait_ge(cc, 1)
                sc.dma_start(out=gath[:, :, :],
                             in_=cout.ap().rearrange("(r p) c -> r p c", p=128)
                             ).then_inc(d_gath, 16)
                sc.wait_ge(pe_car, 1)
                sc.copy(out=carry[:, :], in_=pcar[:, :]).then_inc(act_car, 1)
                for e in range(NE):
                    p = e % 2
                    _wge(sc, dve_stt, e - 1)
                    for t4 in range(NT4):
                        idx = 4 * e + t4
                        sc.wait_ge(pe_q, idx + 1)
                        sc.activation(out=qsb[:, p, 512 * t4:512 * (t4 + 1)],
                                      in_=pq[:, t4, :], func=AF.Copy,
                                      scale=DEN).then_inc(act_q, 1)

            @b3.tensor
            def _(t):
                if ph_max < 3:
                    return
                t.wait_ge(d_gath, 16)
                t.wait_ge(s_small, 80)
                ins = None
                for c in range(NE):
                    ins = t.matmul(pcar[:, c:c + 1], gath[:, :, c], sel_sb[:, :],
                                   start=True, stop=True)
                ins.then_inc(pe_car, 1)
                for e in range(NE):
                    p = e % 2
                    t.wait_ge(ld_wq[p], 16 * (e // 2 + 1))
                    for t4 in range(NT4):
                        idx = 4 * e + t4
                        _wge(t, act_q, idx - 3)
                        cols = slice(512 * t4, 512 * (t4 + 1))
                        ins = None
                        for d in range(ND):
                            ins = t.matmul(pq[:, t4, :], wqb[:, p, d, :], xn[:, d, cols],
                                           start=(d == 0), stop=(d == ND - 1))
                        ins.then_inc(pe_q, 1)

            @b3.vector
            def _(v):
                if ph_max < 3:
                    return
                v.wait_ge(act_car, 1)
                for e in range(NE):
                    p = e % 2
                    v.wait_ge(act_q, 4 * e + 4)
                    v.scalar_tensor_tensor(mix[:, e, :], mix[:, e, :], carry[:, e:e + 1],
                                           qsb[:, p, :], ALU.add,
                                           ALU.mult).then_inc(dve_stt, 1)

        ctx_wqb.__exit__(None, None, None)

    if debug and ph_max >= 3:
        with nc.semaphore("dbg2") as dbg2, nc.Block() as bdbg2:
            @bdbg2.sync
            def _(sy):
                sy.wait_ge(dve_stt, NE)
                sy.wait_ge(act_car, 1)
                sy.dma_start(out=dbg_vhat[:, :, :], in_=mix[:, :, :]).then_inc(dbg2, 16)
                sy.dma_start(out=dbg_car[:, :], in_=carry[:, :]).then_inc(dbg2, 16)
                sy.wait_ge(dbg2, 32)

    fuse0 = ctx.enter_context(nc.sbuf_tensor("fuse0", [128, NM, 512], BF16, side="right"))
    wdb = ctx.enter_context(nc.sbuf_tensor("wdb", [128, 4, 4, 512], BF16, side="right"))
    if True:
        with (
            nc.sbuf_tensor("wob", [128, 2, NE, 512], BF16) as wob,
            nc.sbuf_tensor("s_stage", [128, 4, 512], F32) as s_stage,
            nc.psum_tensor("ps", [128, 4, 512], F32) as ps,
            nc.Block() as b3b,
        ):
            @b3b.sync
            def _(sy):
                if ph_max < 5:
                    return
                sy.wait_ge(st_fuse[0], 16 * 128)
                sy.wait_ge(st_fuse[1], 16 * 128)
                sy.wait_ge(ld_wo[0], 16)
                sy.dma_start(out=fuse0[:, :, :],
                             in_=fuse_d[0]).then_inc(ld_fuse[0], 16)

            @b3b.gpsimd
            def _(g):
                if ph_max < 4:
                    return
                for fg in range(NFG):
                    p = fg % 2
                    _wge(g, pe_s, 16 * (fg - 1))
                    g.dma_start(out=wob[:, p], in_=wo_p[fg]).then_inc(ld_wo[p], 16)
                if ph_max >= 5:
                    for b5i in range(4):
                        g.dma_start(out=wdb[:, b5i],
                                    in_=wd_p[0, b5i]).then_inc(ld_wd[b5i], 16)

            @b3b.tensor
            def _(t):
                if ph_max < 4:
                    return
                t.wait_ge(dve_stt, NE)
                for fg in range(NFG):
                    p = fg % 2
                    t.wait_ge(ld_wo[p], 16 * (fg // 2 + 1))
                    for tt in range(NT):
                        idx = 16 * fg + tt
                        r = tt % 4
                        _wge(t, act_s, idx - 3)
                        ins = None
                        for e in range(NE):
                            ins = t.matmul(ps[:, r, :], mix[:, e, 128 * tt:128 * (tt + 1)],
                                           wob[:, p, e, :], start=(e == 0),
                                           stop=(e == NE - 1))
                        ins.then_inc(pe_s, 1)

            @b3b.scalar
            def _(sc):
                if ph_max < 4:
                    return
                for fg in range(NFG):
                    for tt in range(NT):
                        idx = 16 * fg + tt
                        r = tt % 4
                        sc.wait_ge(pe_s, idx + 1)
                        _wge(sc, st_s[r], 16 * (4 * fg + tt // 4))
                        sc.copy(out=s_stage[:, r, :], in_=ps[:, r, :]).then_inc(act_s, 1)
                        sc.dma_start(out=s_d[128 * tt:128 * (tt + 1),
                                             512 * fg:512 * (fg + 1)],
                                     in_=s_stage[:, r, :]).then_inc(st_s[r], 16)

        mix_guard.__exit__(None, None, None)

        fuse1 = ctx.enter_context(nc.sbuf_tensor("fuse1", [128, NM, 512], BF16, side="right"))
        fuse_bufs = (fuse0, fuse1)
        with (
            nc.sbuf_tensor("s_tiles", [128, 2, 4, 512], F32) as s_tiles,
            nc.sbuf_tensor("x_tiles", [128, 2, 4, 512], F32) as x_tiles,
            nc.sbuf_tensor("tmp5", [128, 2, 512], F32) as tmp5,
            nc.sbuf_tensor("gt5", [128, 2, 512], F32) as gt5,
            nc.sbuf_tensor("gs5", [128, 512], F32) as gs5,
            nc.sbuf_tensor("out_st", [128, 2, 512], F32) as out_st,
            nc.psum_tensor("pdn0", [128, 4, 512], F32) as pdn0,
            nc.psum_tensor("pdn1", [128, 4, 512], F32) as pdn1,
            nc.Block() as b5,
        ):
            pdns = (pdn0, pdn1)
            @b5.gpsimd
            def _(g):
                if ph_max < 5:
                    return
                for t4 in range(NT4):
                    for fg in range(NFG):
                        g5 = 4 * t4 + fg
                        for ob in range(NOB):
                            b5i = 16 * g5 + ob
                            if b5i < 4:
                                continue
                            p2 = b5i % 4
                            _wge(g, pe_dnblk, b5i - 3)
                            g.dma_start(out=wdb[:, p2],
                                        in_=wd_p[fg, ob]).then_inc(ld_wd[p2], 16)

            @b5.scalar
            def _(sc):
                if ph_max < 5:
                    return
                for r in range(4):
                    sc.wait_ge(st_s[r], 16 * 16)
                for tt2 in range(4):
                    sc.dma_start(out=s_tiles[:, 0, tt2, :],
                                 in_=s_d[128 * tt2:128 * (tt2 + 1), 0:512]).then_inc(ld_sx[0], 16)
                    sc.dma_start(out=x_tiles[:, 0, tt2, :],
                                 in_=x[128 * tt2:128 * (tt2 + 1), 0:512]).then_inc(ld_sx[0], 16)
                for t4 in range(NT4):
                    for fg in range(NFG):
                        g5 = 4 * t4 + fg
                        if g5 + 1 < 16:
                            nt4g = (g5 + 1) // 4; nfg = (g5 + 1) % 4
                            pn = (g5 + 1) % 2
                            _wge(sc, dve_out, 4 * g5)
                            for tt2 in range(4):
                                trow = 128 * (4 * nt4g + tt2)
                                fcol = 512 * nfg
                                sc.dma_start(out=s_tiles[:, pn, tt2, :],
                                             in_=s_d[trow:trow + 128, fcol:fcol + 512]
                                             ).then_inc(ld_sx[pn], 16)
                                sc.dma_start(out=x_tiles[:, pn, tt2, :],
                                             in_=x[trow:trow + 128, fcol:fcol + 512]
                                             ).then_inc(ld_sx[pn], 16)
                        for tt2 in range(4):
                            idx5 = 4 * g5 + tt2
                            sc.wait_ge(dve_tmp, idx5 + 1)
                            _wge(sc, dve_out, idx5 - 1)
                            sc.activation(out=gt5[:, tt2 % 2, :], in_=tmp5[:, tt2 % 2, :],
                                          func=AF.Silu).then_inc(act_g, 1)

            @b5.tensor
            def _(t):
                if ph_max < 5:
                    return
                for t4 in range(NT4):
                    t.wait_ge(ld_fuse[t4 % 2], 16 * (t4 // 2 + 1))
                    fsb = fuse_bufs[t4 % 2]
                    for fg in range(NFG):
                        g5 = 4 * t4 + fg
                        pd = pdns[g5 % 2]
                        _wge(t, dve_out, 4 * (g5 - 1))
                        for ob in range(NOB):
                            b5i = 16 * g5 + ob
                            p2 = b5i % 4
                            t.wait_ge(ld_wd[p2], 16 * (b5i // 4 + 1))
                            last_ob = ob == NOB - 1
                            ins = None
                            for oj in range(4):
                                for tt2 in range(4):
                                    ins = t.matmul(
                                        pd[:, tt2, :],
                                        fsb[:, 4 * ob + oj, 128 * tt2:128 * (tt2 + 1)],
                                        wdb[:, p2, oj, :],
                                        start=(ob == 0 and oj == 0),
                                        stop=(last_ob and oj == 3))
                            ins.then_inc(pe_dnblk, 1)

            @b5.vector
            def _(v):
                if ph_max < 5:
                    return
                for t4 in range(NT4):
                    for fg in range(NFG):
                        g5 = 4 * t4 + fg
                        p5 = g5 % 2
                        pd = pdns[g5 % 2]
                        v.wait_ge(pe_dnblk, 16 * (g5 + 1))
                        v.wait_ge(ld_sx[p5], 128 * (g5 // 2 + 1))
                        for tt2 in range(4):
                            idx5 = 4 * g5 + tt2
                            _wge(v, act_g, idx5 - 1)
                            v.tensor_mul(tmp5[:, tt2 % 2, :], pd[:, tt2, :],
                                         sdb[:, 512 * fg:512 * (fg + 1)]).then_inc(dve_tmp, 1)
                            v.wait_ge(act_g, idx5 + 1)
                            v.tensor_mul(gs5[:, :], gt5[:, tt2 % 2, :],
                                         s_tiles[:, p5, tt2, :])
                            _wge(v, st_out[tt2 % 2], 16 * ((idx5 - tt2 % 2) // 2))
                            v.tensor_add(out_st[:, tt2 % 2, :], gs5[:, :],
                                         x_tiles[:, p5, tt2, :]).then_inc(dve_out, 1)

            @b5.sync
            def _(sy):
                if ph_max < 5:
                    return
                sy.dma_start(out=fuse1[:, :, :],
                             in_=fuse_d[1]).then_inc(ld_fuse[1], 16)
                for t4 in range(NT4):
                    for fg in range(NFG):
                        g5 = 4 * t4 + fg
                        if fg == 2 and 2 <= t4 + 1 < NT4:
                            sy.wait_ge(pe_dnblk, 64 * t4)
                            sy.dma_start(out=fuse_bufs[(t4 + 1) % 2][:, :, :],
                                         in_=fuse_d[t4 + 1]).then_inc(
                                             ld_fuse[(t4 + 1) % 2], 16)
                        for tt2 in range(4):
                            idx5 = 4 * g5 + tt2
                            sy.wait_ge(dve_out, idx5 + 1)
                            trow = 128 * (4 * t4 + tt2)
                            fcol = 512 * fg
                            sy.dma_start(out=out[trow:trow + 128, fcol:fcol + 512],
                                         in_=out_st[:, tt2 % 2, :]).then_inc(st_out[tt2 % 2], 16)

                sy.wait_ge(st_out[0], 16 * 32)
                sy.wait_ge(st_out[1], 16 * 32)

    ctx.close()
    return nc


def _pack_inputs(L0_attn_x, L0_attn_Wq, L0_attn_Wk, L0_attn_Wv, L0_attn_Wo,
                 L0_ffn_w_gate, L0_ffn_w_up, L0_ffn_w_down,
                 L0_ffn_s_gate, L0_ffn_s_up, L0_ffn_s_down):
    c = np.ascontiguousarray

    def pack_dxe(W):
        WT = c(np.asarray(W, np.float32).T)
        return c(WT.reshape(ND, 128, NE, 128).transpose(2, 1, 0, 3))

    wq_p = pack_dxe(L0_attn_Wq)
    wk_p = pack_dxe(L0_attn_Wk)
    wv_p = pack_dxe(L0_attn_Wv)
    WoT = c(np.asarray(L0_attn_Wo, np.float32).T)
    wo_p = c(WoT.reshape(NE, 128, NFG, 512).transpose(2, 1, 0, 3))
    wg = np.asarray(L0_ffn_w_gate, np.int8)
    wu = np.asarray(L0_ffn_w_up, np.int8)
    wd = np.asarray(L0_ffn_w_down, np.int8)
    wg_p = c(wg.reshape(ND, 128, NM, 128).transpose(2, 1, 0, 3))
    wu_p = c(wu.reshape(ND, 128, NM, 128).transpose(2, 1, 0, 3))
    wd_p = c(wd.reshape(NOB, 4, 128, NFG, 512).transpose(3, 0, 2, 1, 4))
    sgt = c(np.asarray(L0_ffn_s_gate, np.float32).reshape(NM, 128).T)
    sut = c(np.asarray(L0_ffn_s_up, np.float32).reshape(NM, 128).T)
    sdr = c(np.broadcast_to(np.asarray(L0_ffn_s_down, np.float32).reshape(1, D),
                            (128, D)))
    ident = np.eye(128, dtype=ml_dtypes.bfloat16)
    x_full = np.asarray(L0_attn_x, np.float32)

    in_maps = []
    for core in range(8):
        b, half = core // 2, core % 2
        xc = c(x_full[b, half * T:(half + 1) * T, :])
        selv = np.zeros((8, 1), np.float32)
        if half == 1:
            selv[core - 1, 0] = 1.0
        in_maps.append({
            "x": xc, "wq_p": wq_p, "wk_p": wk_p, "wv_p": wv_p, "wo_p": wo_p,
            "wg_p": wg_p, "wu_p": wu_p, "wd_p": wd_p,
            "sgt": sgt, "sut": sut, "sdr": sdr, "sel": selv, "ident": ident,
        })
    return in_maps


def kernel(L0_attn_x, L0_attn_Wq, L0_attn_Wk, L0_attn_Wv, L0_attn_Wo,
           L0_ffn_w_gate, L0_ffn_w_up, L0_ffn_w_down,
           L0_ffn_s_gate, L0_ffn_s_up, L0_ffn_s_down):
    in_maps = _pack_inputs(L0_attn_x, L0_attn_Wq, L0_attn_Wk, L0_attn_Wv, L0_attn_Wo,
                           L0_ffn_w_gate, L0_ffn_w_up, L0_ffn_w_down,
                           L0_ffn_s_gate, L0_ffn_s_up, L0_ffn_s_down)
    nc = build_nc()
    res = run_bass_kernel_spmd(nc, in_maps, core_ids=list(range(8)))
    out = np.empty((4, 4096, 2048), np.float32)
    for core in range(8):
        b, half = core // 2, core % 2
        out[b, half * T:(half + 1) * T, :] = res.results[core]["out"]
    return out


# version 13
# speedup vs baseline: 1.0606x; 1.0086x over previous
import numpy as np
import ml_dtypes
import concourse.bass as bass
import concourse.mybir as mybir
from concourse.bass_utils import run_bass_kernel_spmd

F32 = mybir.dt.float32
BF16 = mybir.dt.bfloat16
I8 = mybir.dt.int8
AF = mybir.ActivationFunctionType
ALU = mybir.AluOpType
AX = mybir.AxisListType.X

D = 2048
T = 2048
INTER = 8192
ND = 16
NE = 16
NT = 16
NT4 = 4
NM = 64
NFG = 4
NOB = 16
DEN = float(np.float32(np.sqrt(np.float32(2048.0))) / np.float32(4096.0))
EPS = 1e-5


def _wge(eng, sem, v):
    if v > 0:
        eng.wait_ge(sem, v)


def build_nc(ph_max=9, ph0_sub=3, ph1_sub=3, debug=False):
    nc = bass.Bass("TRN2", target_bir_lowering=False, detect_race_conditions=False)

    x = nc.dram_tensor("x", [T, D], F32, kind="ExternalInput").ap()
    wq_p = nc.dram_tensor("wq_p", [NE, 128, ND, 128], F32, kind="ExternalInput").ap()
    wk_p = nc.dram_tensor("wk_p", [NE, 128, ND, 128], F32, kind="ExternalInput").ap()
    wv_p = nc.dram_tensor("wv_p", [NE, 128, ND, 128], F32, kind="ExternalInput").ap()
    wo_p = nc.dram_tensor("wo_p", [NFG, 128, NE, 512], F32, kind="ExternalInput").ap()
    wg_p = nc.dram_tensor("wg_p", [NM, 128, ND, 128], I8, kind="ExternalInput").ap()
    wu_p = nc.dram_tensor("wu_p", [NM, 128, ND, 128], I8, kind="ExternalInput").ap()
    wd_p = nc.dram_tensor("wd_p", [NFG, NOB, 128, 4, 512], I8, kind="ExternalInput").ap()
    sgt = nc.dram_tensor("sgt", [128, NM], F32, kind="ExternalInput").ap()
    sut = nc.dram_tensor("sut", [128, NM], F32, kind="ExternalInput").ap()
    sdr = nc.dram_tensor("sdr", [128, D], F32, kind="ExternalInput").ap()
    sel = nc.dram_tensor("sel", [8, 1], F32, kind="ExternalInput").ap()
    ident = nc.dram_tensor("ident", [128, 128], BF16, kind="ExternalInput").ap()
    out = nc.dram_tensor("out", [T, D], F32, kind="ExternalOutput").ap()

    _skind = "ExternalOutput" if debug else "Internal"
    fuse_d = nc.dram_tensor("fuse_d", [NT4, 128, NM, 512], BF16, kind=_skind).ap()
    s_d = nc.dram_tensor("s_d", [T, D], F32, kind=_skind).ap()
    if debug:
        dbg_xn = nc.dram_tensor("dbg_xn", [128, ND, T], BF16, kind="ExternalOutput").ap()
        dbg_mix = nc.dram_tensor("dbg_mix", [128, NE, T], BF16, kind="ExternalOutput").ap()
        dbg_vhat = nc.dram_tensor("dbg_vhat", [128, NE, T], BF16, kind="ExternalOutput").ap()
        dbg_car = nc.dram_tensor("dbg_car", [128, NE], F32, kind="ExternalOutput").ap()
        dbg_sem = S_dbg = None
    cin = nc.dram_tensor("cin", [128, NE], F32)
    cout = nc.dram_tensor("cout", [8 * 128, NE], F32)

    from contextlib import ExitStack
    ctx = ExitStack()
    S = lambda n: ctx.enter_context(nc.semaphore(n))

    s_small = S("s_small")
    ld_x = (S("ld_x0"), S("ld_x1"))
    ph0_stat = S("ph0_stat"); ph0_sqrt = S("ph0_sqrt"); ph0_norm = S("ph0_norm")
    ph0_sq2 = S("ph0_sq2")
    ph0_tr = S("ph0_tr"); ph0_evac = S("ph0_evac")
    ld_wkv = (S("ld_wkv0"), S("ld_wkv1"))
    pe_ph1 = S("pe_ph1"); act_ph1 = S("act_ph1")
    dve_ph1 = S("dve_ph1"); dve_scan = S("dve_scan"); d_cin = S("d_cin"); cc = S("cc")
    ld_wgu = (S("ld_wgu0"), S("ld_wgu1"), S("ld_wgu2"), S("ld_wgu3"))
    pe_ph4 = S("pe_ph4"); act_ph4 = S("act_ph4")
    dve_ph4u = S("dve_ph4u"); dve_ph4f = S("dve_ph4f")
    st_fuse = (S("st_fuse0"), S("st_fuse1"))
    d_gath = S("d_gath"); pe_car = S("pe_car"); act_car = S("act_car")
    ld_wq = (S("ld_wq0"), S("ld_wq1"))
    pe_q = S("pe_q"); act_q = S("act_q"); dve_stt = S("dve_stt")
    ld_wo = (S("ld_wo0"), S("ld_wo1"))
    pe_s = S("pe_s"); act_s = S("act_s")
    st_s = (S("st_s0"), S("st_s1"), S("st_s2"), S("st_s3"))
    ld_fuse = (S("ld_fuse0"), S("ld_fuse1"))
    ld_wd = (S("ld_wd0"), S("ld_wd1"), S("ld_wd2"), S("ld_wd3"))
    pe_dnblk = S("pe_dnblk")
    dve_tmp = S("dve_tmp"); act_g = S("act_g"); dve_out = S("dve_out")
    ld_sx = (S("ld_sx0"), S("ld_sx1"))
    st_out = (S("st_out0"), S("st_out1"))

    sgt_sb = ctx.enter_context(nc.sbuf_tensor("sgt_sb", [128, NM], F32))
    sut_sb = ctx.enter_context(nc.sbuf_tensor("sut_sb", [128, NM], F32))
    sdb = ctx.enter_context(nc.sbuf_tensor("sdb", [128, D], F32))
    ident_sb = ctx.enter_context(nc.sbuf_tensor("ident_sb", [128, 128], BF16))
    sel_sb = ctx.enter_context(nc.sbuf_tensor("sel_sb", [8, 1], F32))
    h_sb = ctx.enter_context(nc.sbuf_tensor("h_sb", [128, NE], F32))
    carry = ctx.enter_context(nc.sbuf_tensor("carry", [128, NE], F32))

    mix_guard = nc.sbuf_tensor("mix", [128, NE, T], BF16)
    mix = mix_guard.__enter__()
    with (
        nc.sbuf_tensor("xn", [128, ND, T], BF16) as xn,
        nc.sbuf_tensor("wgb", [128, 4, ND, 128], BF16) as wgb,
        nc.sbuf_tensor("wub", [128, 4, ND, 128], BF16) as wub,
    ):
        with (
            nc.sbuf_tensor("xt", [128, 2, D], F32) as xt,
            nc.sbuf_tensor("xnt", [128, 2, D], BF16) as xnt,
            nc.sbuf_tensor("sq", [128, D], BF16) as sq,
            nc.sbuf_tensor("st1", [128, 2], F32) as st1,
            nc.sbuf_tensor("stq", [128, 2], F32) as stq,
            nc.sbuf_tensor("stnm", [128, 2], F32) as stnm,
            nc.sbuf_tensor("stv", [128, 2], F32) as stv,
            nc.sbuf_tensor("stsd", [128, 2], F32) as stsd,
            nc.sbuf_tensor("strs", [128, 2], F32) as strs,
            nc.sbuf_tensor("stw", [128, 2], F32) as stw,
            nc.psum_tensor("ptr0", [128, 4, 128], BF16) as ptr0,
            nc.psum_tensor("ptr1", [128, 4, 128], BF16) as ptr1,
            nc.Block() as b0,
        ):
            ptr = (ptr0, ptr1)

            @b0.sync
            def _(sy):
                sy.dma_start(out=sdb[:, :], in_=sdr[:, :]).then_inc(s_small, 16)
                sy.dma_start(out=sgt_sb[:, :], in_=sgt[:, :]).then_inc(s_small, 16)
                sy.dma_start(out=sut_sb[:, :], in_=sut[:, :]).then_inc(s_small, 16)
                sy.dma_start(out=sel_sb[:, :], in_=sel[:, :]).then_inc(s_small, 16)
                sy.dma_start(out=ident_sb[:, :], in_=ident[:, :]).then_inc(s_small, 16)
                for i in range(NT):
                    _wge(sy, ph0_norm, i - 1)
                    _wge(sy, ph0_sq2, 2 * (i - 1))
                    sy.dma_start(out=xt[:, i % 2, :],
                                 in_=x[128 * i:128 * (i + 1), :]).then_inc(ld_x[i % 2], 16)

            @b0.vector
            def _(v):
                for i in range(NT):
                    p = i % 2
                    v.wait_ge(ld_x[p], 16 * (i // 2 + 1))
                    _wge(v, ph0_sqrt, i - 1)
                    v.wait_ge(ph0_sq2, 2 * (i + 1))
                    v.tensor_scalar_mul(stnm[:, p:p + 1], st1[:, p:p + 1], -1.0 / D)
                    v.tensor_mul(stw[:, p:p + 1], stnm[:, p:p + 1], stnm[:, p:p + 1])
                    v.tensor_scalar_mul(stv[:, p:p + 1], stq[:, p:p + 1], 1.0 / D)
                    v.tensor_sub(stv[:, p:p + 1], stv[:, p:p + 1], stw[:, p:p + 1])
                    v.tensor_scalar_add(stv[:, p:p + 1], stv[:, p:p + 1],
                                        EPS).then_inc(ph0_stat, 1)
                    v.wait_ge(ph0_sqrt, i + 1)
                    v.reciprocal(strs[:, p:p + 1], stsd[:, p:p + 1])
                    if ph0_sub >= 2:
                        _wge(v, ph0_tr, 4 * (i - 1))
                    v.tensor_scalar(xnt[:, p, :], xt[:, p, :], stnm[:, p:p + 1],
                                    strs[:, p:p + 1], ALU.add, ALU.mult).then_inc(ph0_norm, 1)
                    if ph0_sub >= 2 and i > 0:
                        for G in range(4):
                            gidx = 4 * (i - 1) + G
                            v.wait_ge(ph0_tr, gidx + 1)
                            ins = None
                            for j in range(4):
                                ins = v.tensor_copy(xn[:, 4 * G + j, 128 * (i - 1):128 * i],
                                                    ptr[gidx % 2][:, j, :])
                            ins.then_inc(ph0_evac, 1)
                if ph0_sub >= 2:
                    for G in range(4):
                        gidx = 4 * 15 + G
                        v.wait_ge(ph0_tr, gidx + 1)
                        ins = None
                        for j in range(4):
                            ins = v.tensor_copy(xn[:, 4 * G + j, 128 * 15:128 * 16],
                                                ptr[gidx % 2][:, j, :])
                        ins.then_inc(ph0_evac, 1)

            @b0.scalar
            def _(sc):
                sc.preload_act_table(AF.Sqrt)

                def sq2_i(i):
                    p = i % 2
                    sc.wait_ge(ld_x[p], 16 * (i // 2 + 1))
                    _wge(sc, ph0_stat, i - 1)
                    sc.activation(out=sq[:, :], in_=xt[:, p, :], func=AF.Square,
                                  accum_out=stq[:, p:p + 1]).then_inc(ph0_sq2, 1)
                    sc.activation(out=sq[:, :], in_=xt[:, p, :], func=AF.Copy,
                                  accum_out=st1[:, p:p + 1]).then_inc(ph0_sq2, 1)

                def sqrt_i(i):
                    p = i % 2
                    sc.wait_ge(ph0_stat, i + 1)
                    _wge(sc, ph0_norm, i - 1)
                    sc.activation(out=stsd[:, p:p + 1], in_=stv[:, p:p + 1],
                                  func=AF.Sqrt).then_inc(ph0_sqrt, 1)

                sq2_i(0)
                sqrt_i(0)
                for i in range(NT):
                    if i + 1 < NT:
                        sq2_i(i + 1)
                        sqrt_i(i + 1)


            @b0.tensor
            def _(t):
                t.wait_ge(s_small, 80)
                if ph0_sub >= 2:
                    for i in range(NT):
                        p = i % 2
                        t.wait_ge(ph0_norm, i + 1)
                        for G in range(4):
                            gidx = 4 * i + G
                            _wge(t, ph0_evac, gidx - 1)
                            ins = None
                            for j in range(4):
                                n = 4 * G + j
                                ins = t.transpose(ptr[gidx % 2][:, j, :],
                                                  xnt[:, p, 128 * n:128 * (n + 1)],
                                                  ident_sb[:, :])
                            ins.then_inc(ph0_tr, 1)

        with (
            nc.sbuf_tensor("kf32", [128, 2, T], BF16) as kf32,
            nc.sbuf_tensor("kvf32", [128, 2, T], BF16) as kvf32,
            nc.sbuf_tensor("wkb", [128, 2, ND, 128], BF16) as wkb,
            nc.sbuf_tensor("wvb", [128, 2, ND, 128], BF16) as wvb,
            nc.psum_tensor("kpA", [128, 2, 512], F32) as kpA,
            nc.psum_tensor("vpA", [128, 2, 512], F32) as vpA,
            nc.psum_tensor("kpB", [128, 2, 512], F32) as kpB,
            nc.psum_tensor("vpB", [128, 2, 512], F32) as vpB,
            nc.Block() as b1,
        ):
            kp = (kpA, kpB); vp = (vpA, vpB)

            @b1.gpsimd
            def _(g):
                if ph_max < 1:
                    return
                for e in range(NE):
                    p = e % 2
                    _wge(g, pe_ph1, 2 * (e - 1))
                    g.dma_start(out=wkb[:, p], in_=wk_p[e]).then_inc(ld_wkv[p], 16)
                    g.dma_start(out=wvb[:, p], in_=wv_p[e]).then_inc(ld_wkv[p], 16)
                if ph_max >= 2:
                    for m in range(4):
                        g.dma_start(out=wgb[:, m], in_=wg_p[m]).then_inc(ld_wgu[m], 16)
                        g.dma_start(out=wub[:, m], in_=wu_p[m]).then_inc(ld_wgu[m], 16)

            @b1.tensor
            def _(t):
                if ph_max < 1:
                    return
                t.wait_ge(ph0_evac, 64)
                for e in range(NE):
                    p = e % 2
                    t.wait_ge(ld_wkv[p], 32 * (e // 2 + 1))
                    for g2 in range(2):
                        if ph1_sub >= 4:
                            _wge(t, act_ph1, 2 * e + g2)
                            _wge(t, dve_ph1, 2 * e + g2)
                        else:
                            _wge(t, act_ph1, 2 * (e - 1) + g2 + 1)
                            _wge(t, dve_ph1, 2 * (e - 1) + g2 + 1)
                        base = 1024 * g2
                        for d in range(ND):
                            for tt in range(2):
                                t.matmul(kp[g2][:, tt, :], wkb[:, p, d, :],
                                         xn[:, d, base + 512 * tt: base + 512 * (tt + 1)],
                                         start=(d == 0), stop=(d == ND - 1))
                        ins = None
                        for d in range(ND):
                            for tt in range(2):
                                ins = t.matmul(vp[g2][:, tt, :], wvb[:, p, d, :],
                                               xn[:, d, base + 512 * tt: base + 512 * (tt + 1)],
                                               start=(d == 0), stop=(d == ND - 1))
                        ins.then_inc(pe_ph1, 1)

            @b1.scalar
            def _(sc):
                if ph_max < 1:
                    return
                for e in range(NE):
                    p = e % 2
                    _wge(sc, dve_ph1, 2 * (e - 2) + 2)
                    for g2 in range(2):
                        sc.wait_ge(pe_ph1, 2 * e + g2 + 1)
                        base = 1024 * g2
                        sc.copy(out=kf32[:, p, base:base + 512], in_=kp[g2][:, 0, :])
                        sc.copy(out=kf32[:, p, base + 512:base + 1024],
                                in_=kp[g2][:, 1, :]).then_inc(act_ph1, 1)
                sc.wait_ge(dve_scan, NE + 1)
                sc.dma_start(out=cin.ap()[:, :], in_=h_sb[:, :]).then_inc(d_cin, 16)

            @b1.vector
            def _(v):
                if ph_max < 1:
                    return
                for e in range(NE):
                    p = e % 2
                    for g2 in range(2):
                        v.wait_ge(pe_ph1, 2 * e + g2 + 1)
                        v.wait_ge(act_ph1, 2 * e + g2 + 1)
                        base = 1024 * g2
                        if ph1_sub >= 2:
                            v.tensor_mul(kvf32[:, p, base:base + 512],
                                         kf32[:, p, base:base + 512], vp[g2][:, 0, :])
                            v.tensor_mul(kvf32[:, p, base + 512:base + 1024],
                                         kf32[:, p, base + 512:base + 1024],
                                         vp[g2][:, 1, :]).then_inc(dve_ph1, 1)
                        else:
                            v.tensor_copy(kvf32[:, p, base:base + 1024],
                                          kf32[:, p, base:base + 1024]).then_inc(dve_ph1, 1)
                    if ph1_sub >= 3:
                        v.tensor_tensor_scan(mix[:, e, :], kvf32[:, p, :], kvf32[:, p, :],
                                             0.0, ALU.add, ALU.bypass)
                    else:
                        v.tensor_copy(mix[:, e, :], kvf32[:, p, :])
                    v.reduce_sum(h_sb[:, e:e + 1], kvf32[:, p, :],
                                 axis=AX).then_inc(dve_scan, 1)
                v.memset(kvf32[:, 0, 0:8], 0.0).then_inc(dve_scan, 1)

        if debug and ph_max >= 1:
            with nc.semaphore("dbg1") as dbg1, nc.Block() as bdbg1:
                @bdbg1.sync
                def _(sy):
                    sy.wait_ge(ph0_evac, 64)
                    sy.wait_ge(dve_scan, NE)
                    sy.dma_start(out=dbg_xn[:, :, :], in_=xn[:, :, :]).then_inc(dbg1, 16)
                    sy.dma_start(out=dbg_mix[:, :, :], in_=mix[:, :, :]).then_inc(dbg1, 16)
                    sy.wait_ge(dbg1, 32)

        ctx_wqb = nc.sbuf_tensor("wqb", [128, 2, ND, 128], BF16)
        wqb = ctx_wqb.__enter__()
        with (
            nc.sbuf_tensor("gf32", [128, 2, 512], F32) as gf32,
            nc.sbuf_tensor("uf32", [128, 2, 512], F32) as uf32,
            nc.sbuf_tensor("fstage", [128, 2, NT4, 512], BF16) as fstage,
            nc.psum_tensor("pg", [128, 4, 512], F32) as pg,
            nc.psum_tensor("pu", [128, 4, 512], F32) as pu,
            nc.Block() as b4,
        ):
            @b4.gpsimd
            def _(g):
                if ph_max < 2:
                    return
                g.wait_ge(d_cin, 16)
                g.collective_compute(
                    "AllGather", ALU.bypass,
                    replica_groups=[list(range(8))],
                    ins=[cin.ap().opt()], outs=[cout.ap().opt()],
                ).then_inc(cc, 1)
                for m in range(4, NM):
                    p = m % 4
                    _wge(g, pe_ph4, 4 * (m - 3))
                    g.dma_start(out=wgb[:, p], in_=wg_p[m]).then_inc(ld_wgu[p], 16)
                    g.dma_start(out=wub[:, p], in_=wu_p[m]).then_inc(ld_wgu[p], 16)
                if ph_max >= 3:
                    for e in range(2):
                        g.dma_start(out=wqb[:, e], in_=wq_p[e]).then_inc(ld_wq[e], 16)

            @b4.tensor
            def _(t):
                if ph_max < 2:
                    return
                for m in range(NM):
                    p = m % 4
                    t.wait_ge(ld_wgu[p], 32 * (m // 4 + 1))
                    for t4 in range(NT4):
                        idx = 4 * m + t4
                        r = idx % 4
                        _wge(t, act_ph4, idx - 3)
                        _wge(t, dve_ph4u, idx - 3)
                        cols = slice(512 * t4, 512 * (t4 + 1))
                        for d in range(ND):
                            t.matmul(pg[:, r, :], wgb[:, p, d, :], xn[:, d, cols],
                                     start=(d == 0), stop=(d == ND - 1))
                        ins = None
                        for d in range(ND):
                            ins = t.matmul(pu[:, r, :], wub[:, p, d, :], xn[:, d, cols],
                                           start=(d == 0), stop=(d == ND - 1))
                        ins.then_inc(pe_ph4, 1)

            @b4.scalar
            def _(sc):
                if ph_max < 2:
                    return
                sc.preload_act_table(AF.Silu)
                for m in range(NM):
                    for t4 in range(NT4):
                        idx = 4 * m + t4
                        r = idx % 4
                        sc.wait_ge(pe_ph4, idx + 1)
                        _wge(sc, dve_ph4f, idx - 1)
                        sc.activation(out=gf32[:, t4 % 2, :], in_=pg[:, r, :],
                                      func=AF.Silu,
                                      scale=sgt_sb[:, m:m + 1]).then_inc(act_ph4, 1)

            @b4.vector
            def _(v):
                if ph_max < 2:
                    return
                for m in range(NM):
                    p = m % 2
                    for t4 in range(NT4):
                        idx = 4 * m + t4
                        r = idx % 4
                        v.wait_ge(pe_ph4, idx + 1)
                        v.tensor_scalar_mul(uf32[:, t4 % 2, :], pu[:, r, :],
                                            sut_sb[:, m:m + 1]).then_inc(dve_ph4u, 1)
                        v.wait_ge(act_ph4, idx + 1)
                        if t4 == 0:
                            _wge(v, st_fuse[p], 64 * (m // 2))
                        v.tensor_mul(fstage[:, p, t4, :], gf32[:, t4 % 2, :],
                                     uf32[:, t4 % 2, :]).then_inc(dve_ph4f, 1)

            @b4.sync
            def _(sy):
                if ph_max < 2:
                    return
                for m in range(NM):
                    p = m % 2
                    for t4 in range(NT4):
                        idx = 4 * m + t4
                        sy.wait_ge(dve_ph4f, idx + 1)
                        sy.dma_start(out=fuse_d[t4][:, m, :],
                                     in_=fstage[:, p, t4, :]).then_inc(st_fuse[p], 16)

        with (
            nc.sbuf_tensor("qsb", [128, 2, T], BF16) as qsb,
            nc.sbuf_tensor("gath", [8, 128, NE], F32) as gath,
            nc.psum_tensor("pcar", [128, NE], F32) as pcar,
            nc.psum_tensor("pq", [128, 4, 512], F32) as pq,
            nc.Block() as b3,
        ):
            @b3.gpsimd
            def _(g):
                if ph_max < 3:
                    return
                for e in range(2, NE):
                    p = e % 2
                    _wge(g, pe_q, 4 * (e - 1))
                    g.dma_start(out=wqb[:, p], in_=wq_p[e]).then_inc(ld_wq[p], 16)


            @b3.scalar
            def _(sc):
                if ph_max < 3:
                    return
                sc.wait_ge(cc, 1)
                sc.dma_start(out=gath[:, :, :],
                             in_=cout.ap().rearrange("(r p) c -> r p c", p=128)
                             ).then_inc(d_gath, 16)
                sc.wait_ge(pe_car, 1)
                sc.copy(out=carry[:, :], in_=pcar[:, :]).then_inc(act_car, 1)
                for e in range(NE):
                    p = e % 2
                    _wge(sc, dve_stt, e - 1)
                    for t4 in range(NT4):
                        idx = 4 * e + t4
                        sc.wait_ge(pe_q, idx + 1)
                        sc.activation(out=qsb[:, p, 512 * t4:512 * (t4 + 1)],
                                      in_=pq[:, t4, :], func=AF.Copy,
                                      scale=DEN).then_inc(act_q, 1)

            @b3.tensor
            def _(t):
                if ph_max < 3:
                    return
                t.wait_ge(d_gath, 16)
                t.wait_ge(s_small, 80)
                ins = None
                for c in range(NE):
                    ins = t.matmul(pcar[:, c:c + 1], gath[:, :, c], sel_sb[:, :],
                                   start=True, stop=True)
                ins.then_inc(pe_car, 1)
                for e in range(NE):
                    p = e % 2
                    t.wait_ge(ld_wq[p], 16 * (e // 2 + 1))
                    for t4 in range(NT4):
                        idx = 4 * e + t4
                        _wge(t, act_q, idx - 3)
                        cols = slice(512 * t4, 512 * (t4 + 1))
                        ins = None
                        for d in range(ND):
                            ins = t.matmul(pq[:, t4, :], wqb[:, p, d, :], xn[:, d, cols],
                                           start=(d == 0), stop=(d == ND - 1))
                        ins.then_inc(pe_q, 1)

            @b3.vector
            def _(v):
                if ph_max < 3:
                    return
                v.wait_ge(act_car, 1)
                for e in range(NE):
                    p = e % 2
                    v.wait_ge(act_q, 4 * e + 4)
                    v.scalar_tensor_tensor(mix[:, e, :], mix[:, e, :], carry[:, e:e + 1],
                                           qsb[:, p, :], ALU.add,
                                           ALU.mult).then_inc(dve_stt, 1)

        ctx_wqb.__exit__(None, None, None)

    if debug and ph_max >= 3:
        with nc.semaphore("dbg2") as dbg2, nc.Block() as bdbg2:
            @bdbg2.sync
            def _(sy):
                sy.wait_ge(dve_stt, NE)
                sy.wait_ge(act_car, 1)
                sy.dma_start(out=dbg_vhat[:, :, :], in_=mix[:, :, :]).then_inc(dbg2, 16)
                sy.dma_start(out=dbg_car[:, :], in_=carry[:, :]).then_inc(dbg2, 16)
                sy.wait_ge(dbg2, 32)

    fuse0 = ctx.enter_context(nc.sbuf_tensor("fuse0", [128, NM, 512], BF16, side="right"))
    wdb = ctx.enter_context(nc.sbuf_tensor("wdb", [128, 4, 4, 512], BF16, side="right"))
    if True:
        with (
            nc.sbuf_tensor("wob", [128, 2, NE, 512], BF16) as wob,
            nc.sbuf_tensor("s_stage", [128, 4, 512], F32) as s_stage,
            nc.psum_tensor("ps", [128, 4, 512], F32) as ps,
            nc.Block() as b3b,
        ):
            @b3b.sync
            def _(sy):
                if ph_max < 5:
                    return
                sy.wait_ge(st_fuse[0], 16 * 128)
                sy.wait_ge(st_fuse[1], 16 * 128)
                sy.wait_ge(ld_wo[0], 16)
                sy.dma_start(out=fuse0[:, :, :],
                             in_=fuse_d[0]).then_inc(ld_fuse[0], 16)

            @b3b.gpsimd
            def _(g):
                if ph_max < 4:
                    return
                for fg in range(NFG):
                    p = fg % 2
                    _wge(g, pe_s, 16 * (fg - 1))
                    g.dma_start(out=wob[:, p], in_=wo_p[fg]).then_inc(ld_wo[p], 16)
                if ph_max >= 5:
                    for b5i in range(4):
                        g.dma_start(out=wdb[:, b5i],
                                    in_=wd_p[0, b5i]).then_inc(ld_wd[b5i], 16)

            @b3b.tensor
            def _(t):
                if ph_max < 4:
                    return
                t.wait_ge(dve_stt, NE)
                for fg in range(NFG):
                    p = fg % 2
                    t.wait_ge(ld_wo[p], 16 * (fg // 2 + 1))
                    for tt in range(NT):
                        idx = 16 * fg + tt
                        r = tt % 4
                        _wge(t, act_s, idx - 3)
                        ins = None
                        for e in range(NE):
                            ins = t.matmul(ps[:, r, :], mix[:, e, 128 * tt:128 * (tt + 1)],
                                           wob[:, p, e, :], start=(e == 0),
                                           stop=(e == NE - 1))
                        ins.then_inc(pe_s, 1)

            @b3b.scalar
            def _(sc):
                if ph_max < 4:
                    return
                for fg in range(NFG):
                    for tt in range(NT):
                        idx = 16 * fg + tt
                        r = tt % 4
                        sc.wait_ge(pe_s, idx + 1)
                        _wge(sc, st_s[r], 16 * (4 * fg + tt // 4))
                        sc.copy(out=s_stage[:, r, :], in_=ps[:, r, :]).then_inc(act_s, 1)
                        sc.dma_start(out=s_d[128 * tt:128 * (tt + 1),
                                             512 * fg:512 * (fg + 1)],
                                     in_=s_stage[:, r, :]).then_inc(st_s[r], 16)

        mix_guard.__exit__(None, None, None)

        fuse1 = ctx.enter_context(nc.sbuf_tensor("fuse1", [128, NM, 512], BF16, side="right"))
        fuse_bufs = (fuse0, fuse1)
        with (
            nc.sbuf_tensor("s_tiles", [128, 2, 4, 512], F32) as s_tiles,
            nc.sbuf_tensor("x_tiles", [128, 2, 4, 512], F32) as x_tiles,
            nc.sbuf_tensor("tmp5", [128, 2, 512], F32) as tmp5,
            nc.sbuf_tensor("gt5", [128, 2, 512], F32) as gt5,
            nc.sbuf_tensor("gs5", [128, 512], F32) as gs5,
            nc.sbuf_tensor("out_st", [128, 2, 512], F32) as out_st,
            nc.psum_tensor("pdn0", [128, 4, 512], F32) as pdn0,
            nc.psum_tensor("pdn1", [128, 4, 512], F32) as pdn1,
            nc.Block() as b5,
        ):
            pdns = (pdn0, pdn1)
            @b5.gpsimd
            def _(g):
                if ph_max < 5:
                    return
                for t4 in range(NT4):
                    for fg in range(NFG):
                        g5 = 4 * t4 + fg
                        for ob in range(NOB):
                            b5i = 16 * g5 + ob
                            if b5i < 4:
                                continue
                            p2 = b5i % 4
                            _wge(g, pe_dnblk, b5i - 3)
                            g.dma_start(out=wdb[:, p2],
                                        in_=wd_p[fg, ob]).then_inc(ld_wd[p2], 16)

            @b5.scalar
            def _(sc):
                if ph_max < 5:
                    return
                for r in range(4):
                    sc.wait_ge(st_s[r], 16 * 16)
                for tt2 in range(4):
                    sc.dma_start(out=s_tiles[:, 0, tt2, :],
                                 in_=s_d[128 * tt2:128 * (tt2 + 1), 0:512]).then_inc(ld_sx[0], 16)
                    sc.dma_start(out=x_tiles[:, 0, tt2, :],
                                 in_=x[128 * tt2:128 * (tt2 + 1), 0:512]).then_inc(ld_sx[0], 16)
                for t4 in range(NT4):
                    for fg in range(NFG):
                        g5 = 4 * t4 + fg
                        if g5 + 1 < 16:
                            nt4g = (g5 + 1) // 4; nfg = (g5 + 1) % 4
                            pn = (g5 + 1) % 2
                            _wge(sc, dve_out, 4 * g5)
                            for tt2 in range(4):
                                trow = 128 * (4 * nt4g + tt2)
                                fcol = 512 * nfg
                                sc.dma_start(out=s_tiles[:, pn, tt2, :],
                                             in_=s_d[trow:trow + 128, fcol:fcol + 512]
                                             ).then_inc(ld_sx[pn], 16)
                                sc.dma_start(out=x_tiles[:, pn, tt2, :],
                                             in_=x[trow:trow + 128, fcol:fcol + 512]
                                             ).then_inc(ld_sx[pn], 16)
                        for tt2 in range(4):
                            idx5 = 4 * g5 + tt2
                            sc.wait_ge(dve_tmp, idx5 + 1)
                            _wge(sc, dve_out, idx5 - 1)
                            sc.activation(out=gt5[:, tt2 % 2, :], in_=tmp5[:, tt2 % 2, :],
                                          func=AF.Silu).then_inc(act_g, 1)

            @b5.tensor
            def _(t):
                if ph_max < 5:
                    return
                ld_fuse_need = (16, 64, 80, 128)
                for t4 in range(NT4):
                    t.wait_ge(ld_fuse[t4 % 2], ld_fuse_need[t4])
                    fsb = fuse_bufs[t4 % 2]
                    for fg in range(NFG):
                        g5 = 4 * t4 + fg
                        pd = pdns[g5 % 2]
                        _wge(t, dve_out, 4 * (g5 - 1))
                        for ob in range(NOB):
                            b5i = 16 * g5 + ob
                            p2 = b5i % 4
                            t.wait_ge(ld_wd[p2], 16 * (b5i // 4 + 1))
                            last_ob = ob == NOB - 1
                            ins = None
                            for oj in range(4):
                                for tt2 in range(4):
                                    ins = t.matmul(
                                        pd[:, tt2, :],
                                        fsb[:, 4 * ob + oj, 128 * tt2:128 * (tt2 + 1)],
                                        wdb[:, p2, oj, :],
                                        start=(ob == 0 and oj == 0),
                                        stop=(last_ob and oj == 3))
                            ins.then_inc(pe_dnblk, 1)

            @b5.vector
            def _(v):
                if ph_max < 5:
                    return
                for t4 in range(NT4):
                    for fg in range(NFG):
                        g5 = 4 * t4 + fg
                        p5 = g5 % 2
                        pd = pdns[g5 % 2]
                        v.wait_ge(pe_dnblk, 16 * (g5 + 1))
                        v.wait_ge(ld_sx[p5], 128 * (g5 // 2 + 1))
                        for tt2 in range(4):
                            idx5 = 4 * g5 + tt2
                            _wge(v, act_g, idx5 - 1)
                            v.tensor_mul(tmp5[:, tt2 % 2, :], pd[:, tt2, :],
                                         sdb[:, 512 * fg:512 * (fg + 1)]).then_inc(dve_tmp, 1)
                            v.wait_ge(act_g, idx5 + 1)
                            v.tensor_mul(gs5[:, :], gt5[:, tt2 % 2, :],
                                         s_tiles[:, p5, tt2, :])
                            _wge(v, st_out[tt2 % 2], 16 * ((idx5 - tt2 % 2) // 2))
                            v.tensor_add(out_st[:, tt2 % 2, :], gs5[:, :],
                                         x_tiles[:, p5, tt2, :]).then_inc(dve_out, 1)

            @b5.sync
            def _(sy):
                if ph_max < 5:
                    return
                for t4 in range(NT4):
                    for fg in range(NFG):
                        g5 = 4 * t4 + fg
                        for tt2 in range(4):
                            idx5 = 4 * g5 + tt2
                            sy.wait_ge(dve_out, idx5 + 1)
                            trow = 128 * (4 * t4 + tt2)
                            fcol = 512 * fg
                            sy.dma_start(out=out[trow:trow + 128, fcol:fcol + 512],
                                         in_=out_st[:, tt2 % 2, :]).then_inc(st_out[tt2 % 2], 16)
                        if t4 + 1 < NT4 and fg < 3:
                            for q in ((0, 1) if fg == 0 else (fg + 1,)):
                                if t4 + 1 >= 2:
                                    sy.wait_ge(pe_dnblk, 64 * t4)
                                sy.dma_start(
                                    out=fuse_bufs[(t4 + 1) % 2][:, 16 * q:16 * (q + 1), :],
                                    in_=fuse_d[t4 + 1][:, 16 * q:16 * (q + 1), :]
                                ).then_inc(ld_fuse[(t4 + 1) % 2], 16)

                sy.wait_ge(st_out[0], 16 * 32)
                sy.wait_ge(st_out[1], 16 * 32)

    ctx.close()
    return nc


def _pack_inputs(L0_attn_x, L0_attn_Wq, L0_attn_Wk, L0_attn_Wv, L0_attn_Wo,
                 L0_ffn_w_gate, L0_ffn_w_up, L0_ffn_w_down,
                 L0_ffn_s_gate, L0_ffn_s_up, L0_ffn_s_down):
    c = np.ascontiguousarray

    def pack_dxe(W):
        WT = c(np.asarray(W, np.float32).T)
        return c(WT.reshape(ND, 128, NE, 128).transpose(2, 1, 0, 3))

    wq_p = pack_dxe(L0_attn_Wq)
    wk_p = pack_dxe(L0_attn_Wk)
    wv_p = pack_dxe(L0_attn_Wv)
    WoT = c(np.asarray(L0_attn_Wo, np.float32).T)
    wo_p = c(WoT.reshape(NE, 128, NFG, 512).transpose(2, 1, 0, 3))
    wg = np.asarray(L0_ffn_w_gate, np.int8)
    wu = np.asarray(L0_ffn_w_up, np.int8)
    wd = np.asarray(L0_ffn_w_down, np.int8)
    wg_p = c(wg.reshape(ND, 128, NM, 128).transpose(2, 1, 0, 3))
    wu_p = c(wu.reshape(ND, 128, NM, 128).transpose(2, 1, 0, 3))
    wd_p = c(wd.reshape(NOB, 4, 128, NFG, 512).transpose(3, 0, 2, 1, 4))
    sgt = c(np.asarray(L0_ffn_s_gate, np.float32).reshape(NM, 128).T)
    sut = c(np.asarray(L0_ffn_s_up, np.float32).reshape(NM, 128).T)
    sdr = c(np.broadcast_to(np.asarray(L0_ffn_s_down, np.float32).reshape(1, D),
                            (128, D)))
    ident = np.eye(128, dtype=ml_dtypes.bfloat16)
    x_full = np.asarray(L0_attn_x, np.float32)

    in_maps = []
    for core in range(8):
        b, half = core // 2, core % 2
        xc = c(x_full[b, half * T:(half + 1) * T, :])
        selv = np.zeros((8, 1), np.float32)
        if half == 1:
            selv[core - 1, 0] = 1.0
        in_maps.append({
            "x": xc, "wq_p": wq_p, "wk_p": wk_p, "wv_p": wv_p, "wo_p": wo_p,
            "wg_p": wg_p, "wu_p": wu_p, "wd_p": wd_p,
            "sgt": sgt, "sut": sut, "sdr": sdr, "sel": selv, "ident": ident,
        })
    return in_maps


def kernel(L0_attn_x, L0_attn_Wq, L0_attn_Wk, L0_attn_Wv, L0_attn_Wo,
           L0_ffn_w_gate, L0_ffn_w_up, L0_ffn_w_down,
           L0_ffn_s_gate, L0_ffn_s_up, L0_ffn_s_down):
    in_maps = _pack_inputs(L0_attn_x, L0_attn_Wq, L0_attn_Wk, L0_attn_Wv, L0_attn_Wo,
                           L0_ffn_w_gate, L0_ffn_w_up, L0_ffn_w_down,
                           L0_ffn_s_gate, L0_ffn_s_up, L0_ffn_s_down)
    nc = build_nc()
    res = run_bass_kernel_spmd(nc, in_maps, core_ids=list(range(8)))
    out = np.empty((4, 4096, 2048), np.float32)
    for core in range(8):
        b, half = core // 2, core % 2
        out[b, half * T:(half + 1) * T, :] = res.results[core]["out"]
    return out
